# version 17
# baseline (speedup 1.0000x reference)
import numpy as np
import ml_dtypes
import concourse.bass as bass
import concourse.mybir as mybir
from concourse.bass_utils import run_bass_kernel_spmd

F32 = mybir.dt.float32
BF16 = mybir.dt.bfloat16
AF = mybir.ActivationFunctionType
ALU = mybir.AluOpType

D = 4096
SEQ = 4096
NB = 2
NH = 32
NKV = 8
HD = 128
TOK = 1024
NCT = D // 128
BLK = 1024
NBLK = SEQ // BLK
RSCALE = float(1.0 / np.sqrt(HD))
EPS = 1e-5
ENGS = ("sync", "tensor", "scalar", "vector", "gpsimd")
WRING = 8
WRING_DE = 6
ABOUND = 192
XRING = 4



def _perm_rows(n_heads):
    p = np.empty(128, dtype=np.int64)
    p[:64] = np.arange(64) * 2
    p[64:] = np.arange(64) * 2 + 1
    return (np.arange(n_heads)[:, None] * 128 + p[None, :]).reshape(-1)


def _stream(wT):
    c, O = wT.shape
    nog = O // 256
    a = wT.reshape(NCT, 128, nog, 256).transpose(2, 0, 1, 3)
    a = a.reshape(nog, 8, 4, 128, 256).reshape(nog * 8, 4, 128, 256)
    return np.ascontiguousarray(a)


def _host_prep(inputs):
    x = np.asarray(inputs["L0_attn_x"], dtype=np.float32)
    wqkv = np.asarray(inputs["L0_attn_w_qkv"], dtype=np.float32)
    bqkv = np.asarray(inputs["L0_attn_b_qkv"], dtype=np.float32)
    wo = np.asarray(inputs["L0_attn_w_o"], dtype=np.float32)
    bo = np.asarray(inputs["L0_attn_b_o"], dtype=np.float32)
    wup = np.asarray(inputs["L0_ffn_W_up"], dtype=np.float32)
    gw = np.asarray(inputs["L0_ffn_gate_w"], dtype=np.float32)
    eb = np.asarray(inputs["L0_ffn_expert_bias"], dtype=np.float32)
    la = np.asarray(inputs["L0_ffn_lora_A"], dtype=np.float32)
    lb = np.asarray(inputs["L0_ffn_lora_B"], dtype=np.float32)

    pq = _perm_rows(NH)
    pk = _perm_rows(NKV)
    wq = wqkv[0:4096][pq]
    wkvq = np.concatenate([wqkv[4096:5120][pk], wqkv[5120:6144], wq], axis=0)

    shared = {
        "wqkv_s": _stream(np.ascontiguousarray(wkvq.T)),
        "wo_s": _stream(np.ascontiguousarray(wo.T)),
        "wup_s": _stream(np.ascontiguousarray(wup.T)),
        "bqkv_row": np.ascontiguousarray(
            np.concatenate([bqkv[4096:5120][pk], bqkv[5120:6144], bqkv[0:4096][pq]]))[None, :],
        "bo_row": np.ascontiguousarray(bo)[None, :],
        "eb_row": np.ascontiguousarray(eb)[None, :],
        "eb_col": np.concatenate([np.zeros((32, 1), np.float32),
                                  np.ascontiguousarray(eb)[:, None],
                                  np.zeros((28, 1), np.float32)]),
        "lga_re": np.ascontiguousarray(np.concatenate([
            la.reshape(32, D).T.reshape(NCT, 128, 32).transpose(1, 0, 2),
            gw.T.reshape(NCT, 128, 4).transpose(1, 0, 2)], axis=2).reshape(128, NCT * 36)),
        "lb_s": np.ascontiguousarray(lb.transpose(0, 2, 1).reshape(32, D)),
    }

    pos = np.arange(SEQ, dtype=np.float64)
    invf = 1.0 / (10000.0 ** (np.arange(0, HD, 2, dtype=np.float64) / HD))
    ang = invf[:, None] * pos[None, :]
    cosj = np.cos(ang).astype(np.float32)
    sinj = np.sin(ang).astype(np.float32)
    cs_full = np.ascontiguousarray(np.concatenate([cosj, sinj], axis=0))
    sc_full = np.ascontiguousarray(np.concatenate([sinj, cosj], axis=0))
    A = np.zeros((128, 128), dtype=np.float32)
    Bm = np.zeros((128, 128), dtype=np.float32)
    j = np.arange(64)
    A[j, j] = 1.0
    A[j, 64 + j] = -1.0
    Bm[64 + j, j] = 1.0
    Bm[64 + j, 64 + j] = 1.0
    ab = np.ascontiguousarray(np.concatenate([A.T, Bm.T], axis=1))
    mext = (np.arange(896)[None, :] - 384 >= np.arange(128)[:, None]).astype(np.float32)
    e_all = np.zeros((1, 128), dtype=np.float32)
    for e in range(4):
        e_all[0, 32 * e + 8 * e: 32 * e + 8 * e + 8] = 1.0
    shared.update({
        "cs_full": cs_full, "sc_full": sc_full, "ab": ab,
        "mext": np.ascontiguousarray(mext),
        "e_all": e_all, "id128": np.eye(128, dtype=np.float32),
        "ones_row": np.ones((1, 1024), dtype=np.float32),
        "ones128_row": np.ones((1, 128), dtype=np.float32),
        "ones_col": np.ones((128, 1), dtype=np.float32),
        "eps_c": np.full((1, 1), 1e-5, dtype=np.float32),
    })

    per_core = []
    xT = [np.ascontiguousarray(x[b].T) for b in range(NB)]
    for c in range(8):
        b, p = c // 4, c % 4
        cols = np.concatenate([np.arange(512 * p, 512 * p + 512),
                               np.arange(512 * (7 - p), 512 * (7 - p) + 512)])
        m = dict(shared)
        m["xT"] = xT[b]
        m["x_ownT"] = np.ascontiguousarray(xT[b][:, cols])
        m["cs_own"] = np.ascontiguousarray(cs_full[:, cols])
        m["sc_own"] = np.ascontiguousarray(sc_full[:, cols])
        per_core.append(m)
    return per_core



class Ctr:
    def __init__(self, sem, step=1):
        self.sem, self.step, self.n = sem, step, 0


class Plan:

    def __init__(self):
        self.q = {e: [] for e in ENGS}
        self.cur = self.q
        self.autos = {}

    def on(self, e, fn, inc=None):
        auto = self.autos.get(e)
        if auto is not None and (inc is None or inc is auto):
            if auto.n > 0:
                self.cur[e].append(lambda eng, s2=auto.sem, v2=auto.n: eng.wait_ge(s2, v2))
            auto.n += 1
            v = auto.n
            self.cur[e].append(lambda eng, s2=auto.sem: fn(eng).then_inc(s2, 1))
            return v
        if auto is not None and auto.n > 0:
            self.cur[e].append(lambda eng, s2=auto.sem, v2=auto.n: eng.wait_ge(s2, v2))
        if inc is None:
            self.cur[e].append(fn)
            return None
        inc.n += inc.step
        v = inc.n
        sem, st = inc.sem, inc.step
        self.cur[e].append(lambda eng: fn(eng).then_inc(sem, st))
        return v

    def raw(self, e, fn):
        self.cur[e].append(fn)

    def wait(self, e, ctr, val):
        if val is None or val <= 0:
            return
        self.cur[e].append(lambda eng, s=ctr.sem, v=val: eng.wait_ge(s, v))


def build_program():
    nc = bass.Bass("TRN2", target_bir_lowering=False)

    def din(name, shape):
        return nc.dram_tensor(name, list(shape), F32, kind="ExternalInput").ap()

    h_xT = din("xT", (D, SEQ))
    h_xown = din("x_ownT", (D, TOK))
    h_wqkv = din("wqkv_s", (192, 4, 128, 256))
    h_wo = din("wo_s", (128, 4, 128, 256))
    h_wup = din("wup_s", (128, 4, 128, 256))
    h_bqkv = din("bqkv_row", (1, 6144))
    h_bo = din("bo_row", (1, 4096))
    h_eb = din("eb_row", (1, 4))
    h_ebc = din("eb_col", (64, 1))
    h_lga = din("lga_re", (128, NCT * 36))
    h_lb = din("lb_s", (32, D))
    h_csf = din("cs_full", (128, SEQ))
    h_scf = din("sc_full", (128, SEQ))
    h_cso = din("cs_own", (128, TOK))
    h_sco = din("sc_own", (128, TOK))
    h_ab = din("ab", (128, 256))
    h_mext = din("mext", (128, 896))
    h_eall = din("e_all", (1, 128))
    h_id = din("id128", (128, 128))
    h_ones = din("ones_row", (1, 1024))
    h_ones128 = din("ones128_row", (1, 128))
    h_onescol = din("ones_col", (128, 1))
    h_eps = din("eps_c", (1, 1))
    h_out = nc.dram_tensor("outT", [D, TOK], F32, kind="ExternalOutput").ap()

    s_kl = nc.dram_tensor("k_loc", [NKV * 128, TOK], BF16, kind="Internal").ap()
    s_vl = nc.dram_tensor("v_loc", [TOK, NKV * 128], BF16, kind="Internal").ap()
    s_ka = nc.dram_tensor("k_all", [4 * NKV * 128, TOK], BF16, kind="Internal").ap()
    s_va = nc.dram_tensor("v_all", [4 * TOK, NKV * 128], BF16, kind="Internal").ap()
    s_q = nc.dram_tensor("q_scr", [NH * 128, TOK], BF16, kind="Internal").ap()
    s_x1 = nc.dram_tensor("x1_scr", [D, TOK], F32, kind="Internal").ap()
    s_fin = nc.dram_tensor("fin_scr", [4, 512], F32, kind="Internal").ap()

    from contextlib import ExitStack
    ctx = ExitStack()

    ARENA = 204 * 1024
    arena = nc.alloc_sbuf_tensor("arena", [128, ARENA], mybir.dt.uint8)
    BASE = nc.lookup_mloc(arena).addr

    def place(name, shape, dtype, off_kb):
        off = BASE + int(round(off_kb * 1024))
        return nc.alloc_sbuf_tensor_at(name, list(shape), dtype, offset=off).ap()

    buf_h = place("buf_h", [128, NCT * 1024], BF16, 0)
    kstage = place("kstage", [128, NKV * 1024], BF16, 64)
    vstage = place("vstage", [128, 8 * 1024], BF16, 80)
    vtmp = place("vtmp", [128, 2 * 512], BF16, 96)
    c_bqkv = place("c_bqkv", [1, 6144], BF16, 98)
    c_cso = place("c_cso", [128, TOK], BF16, 110)
    c_sco = place("c_sco", [128, TOK], BF16, 112)
    tab = place("tab", [128, 4 * 512], BF16, 118)
    c_ab = place("c_ab", [128, 256], BF16, 122)
    c_id = place("c_id", [128, 128], BF16, 123)
    buf_at = place("buf_at", [128, NCT * 1024], BF16, 64)
    c_lb = place("c_lb", [32, D], BF16, 64)
    c_lga = place("c_lga", [128, NCT * 36], BF16, 72)
    zrow = place("zrow", [32, 1024], BF16, 74.75)
    m32 = place("m32", [32, 1024], BF16, 76.75)
    lgrow = place("lgrow", [64, 1024], F32, 78.75)
    ohr = place("ohr", [1, 4096], BF16, 82.75)
    grows = place("grows", [1, 7168], F32, 90.75)
    wring = place("wring", [128, WRING * 1024], BF16, 128)
    xroll = place("xroll", [128, XRING * 1024], F32, 140)
    x1buf = place("x1buf", [128, 2 * 1024], F32, 156)
    outb = place("outb", [128, 2 * 1024], F32, 156)
    kbuf = place("kbuf", [128, 2 * 4096], BF16, 128)
    vbuf = place("vbuf", [128, 2 * 4096], BF16, 144)
    qbuf = place("qbuf", [128, 2 * 4096], BF16, 160)
    sacc = place("sacc", [128, 1024], F32, 164)
    qacc = place("qacc", [128, 1024], F32, 168)
    xsq = place("xsq", [128, 1024], F32, 172)
    dacc = place("dacc", [128, 4 * 512], F32, 176)
    invden = place("invden", [128, 2 * 512], F32, 184)
    xsq2 = place("xsq2", [128, 1024], F32, 184)
    gtmp = place("gtmp", [128, 1024], F32, 188)
    pbuf = place("pbuf", [128, 4 * 1024], BF16, 188)
    bc_negm = place("bc_negm", [128, 1024], F32, 176)
    bc_inv = place("bc_inv", [128, 1024], F32, 180)
    c_bo = place("c_bo", [1, 4096], BF16, 192)
    rows = place("rows", [1, 2048], F32, 192)
    c_mext = place("c_mext", [128, 896], BF16, 200)
    c_std = place("c_std", [1, 1024], BF16, 114)
    c_ocolb = place("c_ocolb", [128, 1], BF16, 116)
    xsqA = place("xsqA", [128, 1024], BF16, 172)
    xsqB = place("xsqB", [128, 1024], BF16, 184)
    c_ones = place("c_ones", [1, 512], BF16, 202)
    c_o128f = place("c_o128f", [1, 128], F32, 203)
    c_ocol = place("c_ocol", [128, 1], F32, 203.5)
    c_eb = place("c_eb", [1, 4], BF16, 203.625)
    c_ebc = place("c_ebc", [64, 1], F32, 203.53125)
    c_eall = place("c_eall", [1, 128], BF16, 203.75)
    c_eps = place("c_eps", [1, 1], F32, 203.6875)
    htmp = xsq

    psd = [ctx.enter_context(nc.psum_tensor(f"psd{i}", [128, 1024], F32)) for i in range(4)]

    mksem = lambda n: ctx.enter_context(nc.semaphore(n))
    dcst = Ctr(mksem("dcst"), 16)
    dws = [Ctr(mksem(f"dw{i}"), 16) for i in range(WRING)]
    dxs = [Ctr(mksem(f"dx{i}"), 16) for i in range(XRING)]
    dscr = Ctr(mksem("dscr"), 16)
    dvscr = Ctr(mksem("dvscr"), 16)
    dkvs = [Ctr(mksem(f"dkv{i}"), 16) for i in range(2)]
    qds = [Ctr(mksem(f"qd{i}"), 16) for i in range(4)]
    dqf = [Ctr(mksem(f"dqf{i}"), 16) for i in range(4)]
    dx1s = [Ctr(mksem(f"dx1{i}"), 16) for i in range(2)]
    douts = [Ctr(mksem(f"dout{i}"), 16) for i in range(2)]
    dgr = Ctr(mksem("dgr"), 16)
    dfsp = [Ctr(mksem(f"dfsp{i}"), 16) for i in range(4)]
    dfbc = [Ctr(mksem(f"dfbc{i}"), 16) for i in range(4)]
    cG = Ctr(mksem("cG"))
    cT = Ctr(mksem("cT"))
    cTe = Ctr(mksem("cTe"))
    cVe = Ctr(mksem("cVe"))
    cS = Ctr(mksem("cS"))
    cV = Ctr(mksem("cV"))

    P = Plan()
    P.autos = {"vector": cV, "scalar": cS, "gpsimd": cG}

    docb = Ctr(mksem("docb"), 16)
    P.on("gpsimd", lambda g: g.dma_start(c_ocolb[:, :], h_onescol[:, :]), inc=docb)
    for dst, src in [(c_cso, h_cso[:, :]), (c_sco, h_sco[:, :]), (c_ab, h_ab[:, :]),
                     (c_id, h_id[:, :]), (c_mext, h_mext[:, :]), (c_eall, h_eall[:, :]),
                     (c_ones, h_ones[0:1, 0:512]), (c_o128f, h_ones128[:, :]),
                     (c_ocol, h_onescol[:, :]), (c_bqkv, h_bqkv[:, :]), (c_eb, h_eb[:, :]),
                     (c_ebc, h_ebc[:, :]), (c_eps, h_eps[:, :])]:
        P.on("gpsimd", lambda g, d=dst, s=src: g.dma_start(d[:, :], s), inc=dcst)
    CST = dcst.n
    P.wait("tensor", docb, 16)

    chunk_done = {}
    _ci = [0]

    ring_gates = {}
    pre_loaded = {}

    def load_chunk(src_h, li):
        gi = _ci[0]
        _ci[0] += 1
        if gi < ABOUND:
            slot = gi % WRING
            prev_gi = gi - WRING
        else:
            li2 = gi - ABOUND
            slot = li2 % WRING_DE
            prev_gi = gi - WRING_DE if li2 >= WRING_DE else ABOUND - WRING + li2
        rg = ring_gates.pop(gi, None)
        if rg is not None:
            P.wait("gpsimd", rg[0], rg[1])
        if prev_gi >= 0:
            P.wait("gpsimd", cT, chunk_done[prev_gi])
        dst = wring[:, slot * 1024:(slot + 1) * 1024].rearrange("p (j c) -> p j c", c=256)
        src = src_h[li, :, :, :].rearrange("j p c -> p j c")
        wt = P.on("gpsimd", lambda g, d=dst, s=src: g.dma_start(d, s), inc=dws[slot])
        return gi, slot, (dws[slot], wt)

    def wslice(slot, j, ot):
        o = slot * 1024 + j * 256 + ot * 128
        return wring[:, o:o + 128]

    x_done = {}
    _xi = [0]

    def load_x(src_ap, width=1024):
        xi = _xi[0]
        _xi[0] += 1
        slot = xi % XRING
        if xi - XRING >= 0:
            ctr, val = x_done[xi - XRING]
            P.wait("sync", ctr, val)
        wt = P.on("sync", lambda s, d=xroll[:, slot * 1024:slot * 1024 + width], a=src_ap:
                  s.dma_start(d, a), inc=dxs[slot])
        return xi, slot, (dxs[slot], wt)

    def xtw(slot, width):
        return xroll[:, slot * 1024:slot * 1024 + width]

    def xt(slot):
        return xroll[:, slot * 1024:(slot + 1) * 1024]

    holder_stats = [None]
    xsq_user = {0: None, 1: None}
    xsqb = {0: xsq, 1: xsq2}

    def ln_stats(src_fn, width=1024):
        last = None
        for ct in range(NCT):
            par = ct % 2
            xb = xsqb[par]
            xi, slot, xw = load_x(src_fn(ct), width)
            P.wait("vector", xw[0], xw[1])
            if ct == 0:
                P.on("vector", lambda v, sl=slot, w=width: v.tensor_copy(sacc[:, 0:w], xtw(sl, w)))
            else:
                P.on("vector", lambda v, sl=slot, w=width: v.tensor_add(sacc[:, 0:w], sacc[:, 0:w], xtw(sl, w)))
            P.wait("scalar", xw[0], xw[1])
            if xsq_user[par] is not None:
                P.wait("scalar", cV, xsq_user[par])
            sv = P.on("scalar", lambda sc, sl=slot, w=width, x2=xb: sc.activation(x2[:, 0:w], xtw(sl, w), AF.Square), inc=cS)
            P.wait("vector", cS, sv)
            if ct == 0:
                vv = P.on("vector", lambda v, w=width, x2=xb: v.tensor_copy(qacc[:, 0:w], x2[:, 0:w]), inc=cV)
            else:
                vv = P.on("vector", lambda v, w=width, x2=xb: v.tensor_add(qacc[:, 0:w], qacc[:, 0:w], x2[:, 0:w]), inc=cV)
            xsq_user[par] = vv
            x_done[xi] = (cV, vv)
            last = vv
            yield
        holder_stats[0] = last

    stat_par = [0]

    def stat_into(tile_ap, first):
        par = stat_par[0]
        stat_par[0] ^= 1
        xb = xsqb[par]
        if first:
            P.on("vector", lambda v: v.tensor_copy(sacc[:, :], tile_ap))
        else:
            P.on("vector", lambda v: v.tensor_add(sacc[:, :], sacc[:, :], tile_ap))
        if xsq_user[par] is not None:
            P.wait("scalar", cV, xsq_user[par])
        sv = P.on("scalar", lambda sc, x2=xb: sc.activation(x2[:, :], tile_ap, AF.Square), inc=cS)
        P.wait("vector", cS, sv)
        if first:
            vv = P.on("vector", lambda v, x2=xb: v.tensor_copy(qacc[:, :], x2[:, :]), inc=cV)
        else:
            vv = P.on("vector", lambda v, x2=xb: v.tensor_add(qacc[:, :], qacc[:, :], x2[:, :]), inc=cV)
        xsq_user[par] = vv
        holder_stats[0] = vv
        return vv, sv

    def ln_finalize(width=1024, psum_tv=None, mkstd=False):
        nh = width // 512
        if psum_tv is None:
            P.wait("tensor", cV, holder_stats[0])
            tv = None
            for hh in range(nh):
                P.on("tensor", lambda t, h2=hh: t.matmul(
                    psd[2][0:1, h2 * 512:(h2 + 1) * 512], c_ocol[:, :],
                    sacc[:, h2 * 512:(h2 + 1) * 512], start=True, stop=True))
                tv = P.on("tensor", lambda t, h2=hh: t.matmul(
                    psd[3][0:1, h2 * 512:(h2 + 1) * 512], c_ocol[:, :],
                    qacc[:, h2 * 512:(h2 + 1) * 512], start=True, stop=True), inc=cT)
            yield
        else:
            tv = psum_tv
        P.wait("scalar", cT, tv)
        P.on("scalar", lambda sc, w=width: sc.activation(rows[0:1, 0:w], psd[2][0:1, 0:w], AF.Copy, scale=1.0 / D))
        sv = P.on("scalar", lambda sc, w=width: sc.activation(rows[0:1, 1024:1024 + w], psd[3][0:1, 0:w], AF.Copy, scale=1.0 / D), inc=cS)
        P.wait("vector", cS, sv)
        P.on("vector", lambda v, w=width: v.tensor_mul(bc_negm[0:1, 0:w], rows[0:1, 0:w], rows[0:1, 0:w]))
        vv = P.on("vector", lambda v, w=width: v.tensor_sub(
            bc_inv[0:1, 0:w], rows[0:1, 1024:1024 + w], bc_negm[0:1, 0:w]), inc=cV)
        P.wait("scalar", cV, vv)
        P.on("scalar", lambda sc, w=width: sc.activation(bc_negm[0:1, 0:w], bc_inv[0:1, 0:w], AF.Ln, bias=c_eps[0:1, 0:1]))
        P.on("scalar", lambda sc, w=width: sc.activation(
            rows[0:1, 1024:1024 + w], bc_negm[0:1, 0:w], AF.Exp, scale=-0.5))
        if mkstd:
            P.on("scalar", lambda sc, w=width: sc.activation(
                c_std[0:1, 0:w], bc_negm[0:1, 0:w], AF.Exp, scale=0.5))
        sv = P.on("scalar", lambda sc, w=width: sc.activation(bc_inv[0:1, 0:w], rows[0:1, 0:w], AF.Copy, scale=-1.0), inc=cS)
        yield
        P.wait("tensor", cS, sv)
        for hh in range(nh):
            P.on("tensor", lambda t, h2=hh: t.matmul(
                psd[2][:, h2 * 512:(h2 + 1) * 512], c_o128f[:, :],
                bc_inv[0:1, h2 * 512:(h2 + 1) * 512], start=True, stop=True))
            tv = P.on("tensor", lambda t, h2=hh: t.matmul(
                psd[3][:, h2 * 512:(h2 + 1) * 512], c_o128f[:, :],
                rows[0:1, 1024 + h2 * 512:1024 + (h2 + 1) * 512], start=True, stop=True), inc=cT)
        yield
        P.wait("scalar", cT, tv)
        P.on("scalar", lambda sc, w=width: sc.copy(bc_negm[:, 0:w], psd[2][:, 0:w]))
        sv = P.on("scalar", lambda sc, w=width: sc.copy(bc_inv[:, 0:w], psd[3][:, 0:w]), inc=cS)
        holder_fin[0] = sv
        yield

    holder_fin = [None]

    def h_pass(src_fn, dst_fn, bc_ready, width=1024, te_guard=None, bc_off=0):
        P.wait("vector", cS, bc_ready)
        P.wait("gpsimd", cS, bc_ready)
        if te_guard is not None:
            P.wait("vector", cT, te_guard)
            P.wait("gpsimd", cT, te_guard)
        lastv, lastg = None, None
        for ct in range(NCT):
            xi, slot, xw = load_x(src_fn(ct), width)
            eng = "gpsimd" if ct % 3 == 2 else "vector"
            ctr = cG if eng == "gpsimd" else cV
            tb = gtmp if eng == "gpsimd" else htmp
            P.wait(eng, xw[0], xw[1])
            P.on(eng, lambda v, sl=slot, w=width, o=bc_off, t2=tb:
                 v.tensor_add(t2[:, 0:w], xtw(sl, w), bc_negm[:, o:o + w]))
            vv = P.on(eng, lambda v, c2=ct, w=width, o=bc_off, t2=tb:
                      v.tensor_mul(dst_fn(c2), t2[:, 0:w], bc_inv[:, o:o + w]), inc=ctr)
            if eng == "vector":
                xsq_user[0] = vv
                lastv = vv
            else:
                lastg = vv
            x_done[xi] = (ctr, vv)
            yield
        holder_h[0] = lastv
        holder_h[1] = lastg
        yield

    holder_h = [None, None]

    def ln_chain(src_fn, dst_fn, width=1024, te_guard=None):
        for y in ln_stats(src_fn, width):
            yield
        for y in ln_finalize(width):
            yield
        bc = holder_fin[0]
        for y in h_pass(src_fn, dst_fn, bc, width, te_guard):
            yield

    def drain(gen):
        if gen is None:
            return
        for _ in gen:
            pass

    def proj_stream(src_h, nog, rhs_fn, bias, epilogue, extra_acc=None,
                    th_count=2, bg=None, init_guards=None, psd_mode="quad",
                    rhs_ready=None):
        pend = []

        def defer(fn):
            pend.append(fn)

        def flush():
            nonlocal pend
            for fn in pend:
                fn()
            pend = []

        def bg_step(k):
            nonlocal bg
            if bg is None:
                return
            try:
                for _ in range(k):
                    next(bg)
            except StopIteration:
                bg = None

        guards = init_guards if init_guards is not None else {0: None, 1: None}
        for og in range(nog):
            pp = og % 2 if psd_mode == "quad" else 0
            if psd_mode == "quad":
                if th_count == 2:
                    units = {(ot, th): psd[2 * pp + ot][:, th * 512:(th + 1) * 512]
                             for ot in range(2) for th in range(2)}
                else:
                    units = {(ot, 0): psd[pp][:, ot * 512:(ot + 1) * 512] for ot in range(2)}
            else:
                units = {(ot, th): psd[ot][:, th * 512:(th + 1) * 512]
                         for ot in range(2) for th in range(th_count)}

            def start_group():
                if guards[pp] is not None:
                    ctr, val = guards[pp]
                    P.wait("tensor", ctr, val)
                if bias is not None:
                    rowh, coloff = bias[0], bias[1]
                    brhs = bias[2] if len(bias) > 2 else (
                        lambda t2: c_ones[0:1, 0:512])
                    for (ot, th), u in units.items():
                        P.on("tensor", lambda t, u2=u, rh=rowh, t3=th,
                             co=coloff(og, ot): t.matmul(
                                 u2, rh[0:1, co:co + 128], brhs(t3),
                                 start=True, stop=False))

            delayed = (psd_mode == "compact")
            if not delayed:
                start_group()
            stash = []
            tv = None
            nu = len(units)
            for sci in range(8):
                gi, slot, cwt = load_chunk(src_h, og * 8 + sci)
                if sci == 2:
                    flush()
                    if delayed:
                        start_group()
                        for fn in stash:
                            fn()
                        stash = []
                if sci >= 3:
                    bg_step(3)

                def emit_chunk(sci2=sci, gi2=gi, slot2=slot, cwt2=cwt):
                    nonlocal tv
                    if rhs_ready is not None and og == 0:
                        ctr2, val2 = rhs_ready[sci2 * 4 + 3]
                        P.wait("tensor", ctr2, val2)
                    P.wait("tensor", cwt2[0], cwt2[1])
                    for jj in range(4):
                        ct = sci2 * 4 + jj
                        for ui, ((ot, th), u) in enumerate(units.items()):
                            first = (ct == 0 and bias is None)
                            last = (ct == 31 and extra_acc is None)
                            is_tail = (jj == 3 and ui == nu - 1)
                            tv = P.on("tensor", lambda t, sl=slot2, j2=jj, o2=ot, t2=th,
                                      f=first, l=last, c2=ct, u2=u:
                                      t.matmul(u2, wslice(sl, j2, o2), rhs_fn(c2, t2),
                                               start=f, stop=l),
                                      inc=cT if is_tail else None)
                    chunk_done[gi2] = tv

                if delayed and sci < 2:
                    stash.append(emit_chunk)
                else:
                    emit_chunk()
            if extra_acc is not None:
                for (ot, th), u in units.items():
                    tv = extra_acc(og, ot, th, u)
            guards[pp] = epilogue(og, units, tv, defer)
        flush()
        bg_step(1000000)
        return guards

    def rope_factory(cs_const, sc_const, tokoff_fn, stage_dst_fn, pre_evac_wait=None):
        state = {"u": 0, "prev": None}

        def epi(og, units, tv, defer):
            last_guard = None
            for (ot, th), u in units.items():
                up = state["u"] % 2
                ta = tab[:, up * 1024: up * 1024 + 512]
                tb = tab[:, up * 1024 + 512: up * 1024 + 1024]
                to = tokoff_fn(og, th)
                P.wait("vector", cT, tv)
                if state["prev"] is not None:
                    P.wait("vector", cTe, state["prev"])
                P.on("vector", lambda v, u2=u, a=ta, o=to:
                     v.tensor_mul(a, u2, cs_const[:, o:o + 512]))
                vv = P.on("vector", lambda v, u2=u, b2=tb, o=to:
                          v.tensor_mul(b2, u2, sc_const[:, o:o + 512]), inc=cV)
                cTe.n += 2
                trv = cTe.n
                state["prev"] = trv
                cS.n += 1
                sv = cS.n
                dst = stage_dst_fn(og, ot, th)
                sguard = pre_evac_wait(og) if pre_evac_wait else None

                def te_part(u2=u, a=ta, b2=tb, vv2=vv, trv2=trv, sv2=sv, d=dst, sg=sguard):
                    P.raw("tensor", lambda eng: eng.wait_ge(cV.sem, vv2))
                    P.raw("tensor", lambda eng: eng.matmul(
                        u2, c_ab[:, 0:128], a, start=True, stop=False).then_inc(cTe.sem, 1))
                    P.raw("tensor", lambda eng: eng.matmul(
                        u2, c_ab[:, 128:256], b2, start=False, stop=True).then_inc(cTe.sem, 1))
                    P.raw("scalar", lambda eng: eng.wait_ge(cTe.sem, trv2))
                    if sg is not None:
                        P.raw("scalar", lambda eng, s2=sg: eng.wait_ge(s2[0].sem, s2[1]))
                    P.raw("scalar", lambda eng: eng.wait_ge(cS.sem, sv2 - 1))
                    P.raw("scalar", lambda eng: eng.copy(d, u2).then_inc(cS.sem, 1))

                defer(te_part)
                state["u"] += 1
                last_guard = (cS, sv)
            return last_guard

        return epi

    def v_factory():
        state = {"u": 0, "prev": None}
        vview_all = vstage[:, :].rearrange("p (j c) -> p j c", c=1024)

        def epi(og, units, tv, defer):
            last_guard = None
            for (ot, th), u in units.items():
                up = state["u"] % 2
                vt = vtmp[:, up * 512:(up + 1) * 512]
                vo = (og - 4) * 256 + ot * 128
                prev = state["prev"]
                cVe.n += 1
                vev = cVe.n
                cTe.n += 4
                ttv = cTe.n
                state["prev"] = ttv
                cS.n += 1
                sv = cS.n
                dview = vview_all[:, 4 * th:4 * th + 4, vo:vo + 128]
                srcv = u.rearrange("p (j c) -> p j c", c=128)
                isl = bc_inv[:, th * 512:(th + 1) * 512]

                def te_part(u2=u, v2=vt, tv2=tv, pv=prev, vev2=vev, ttv2=ttv,
                            sv2=sv, dv=dview, sv3=srcv, il=isl):
                    P.raw("vector", lambda eng: eng.wait_ge(cT.sem, tv2))
                    if pv is not None:
                        P.raw("vector", lambda eng: eng.wait_ge(cTe.sem, pv))
                    P.raw("vector", lambda eng: eng.tensor_mul(v2, u2, il).then_inc(cVe.sem, 1))
                    P.raw("tensor", lambda eng: eng.wait_ge(cVe.sem, vev2))
                    for jj in range(4):
                        P.raw("tensor", lambda eng, j2=jj: eng.matmul(
                            u2[:, j2 * 128:(j2 + 1) * 128],
                            v2[:, j2 * 128:(j2 + 1) * 128], c_id[:, :],
                            start=True, stop=True).then_inc(cTe.sem, 1))
                    P.raw("scalar", lambda eng: eng.wait_ge(cTe.sem, ttv2))
                    P.raw("scalar", lambda eng: eng.wait_ge(cS.sem, sv2 - 1))
                    P.raw("scalar", lambda eng: eng.copy(dv, sv3).then_inc(cS.sem, 1))

                defer(te_part)
                state["u"] += 1
                last_guard = (cS, sv)
            return last_guard

        return epi

    dag = Ctr(mksem("dag"), 1)
    ag_state = {}
    dst_h = lambda ct: buf_h[:, ct * 1024:(ct + 1) * 1024]
    src_own = lambda ct: h_xown[128 * ct:128 * (ct + 1), :]

    xsqAB = [xsqA, xsqB]
    te_sq = {}
    tvst = None
    for ct in range(NCT):
        xi, slot, xw = load_x(src_own(ct), 1024)
        P.wait("scalar", xw[0], xw[1])
        sv = P.on("scalar", lambda sc, sl=slot, c2=ct: sc.copy(
            buf_h[:, c2 * 1024:(c2 + 1) * 1024], xtw(sl, 1024)), inc=cS)
        x_done[xi] = (cS, sv)
        P.wait("vector", cS, sv)
        if ct >= 2:
            P.wait("vector", cT, te_sq[ct - 2])
        vv = P.on("vector", lambda v, c2=ct, xb=xsqAB[ct % 2]: v.tensor_mul(
            xb[:, :], buf_h[:, c2 * 1024:(c2 + 1) * 1024],
            buf_h[:, c2 * 1024:(c2 + 1) * 1024]), inc=cV)
        P.wait("tensor", cS, sv)
        for th in range(2):
            P.on("tensor", lambda t, c2=ct, t2=th: t.matmul(
                psd[2][0:1, t2 * 512:(t2 + 1) * 512], c_ocolb[:, :],
                buf_h[:, c2 * 1024 + t2 * 512: c2 * 1024 + (t2 + 1) * 512],
                start=(c2 == 0), stop=(c2 == 31)))
        P.wait("tensor", cV, vv)
        for th in range(2):
            tvst = P.on("tensor", lambda t, c2=ct, t2=th, xb=xsqAB[ct % 2]: t.matmul(
                psd[3][0:1, t2 * 512:(t2 + 1) * 512], c_ocolb[:, :],
                xb[:, t2 * 512:(t2 + 1) * 512],
                start=(c2 == 0), stop=(c2 == 31)), inc=cT)
        te_sq[ct] = tvst
    ring_gates[6] = x_done[28]
    ring_gates[7] = x_done[28]
    P.wait("scalar", dcst, CST)
    drain(ln_finalize(1024, psum_tv=tvst, mkstd=True))
    bc2a = holder_fin[0]
    P.wait("vector", dcst, CST)
    P.wait("vector", cS, bc2a)
    P.on("vector", lambda v: v.tensor_mul(c_cso[:, :], c_cso[:, :], bc_inv[:, :]))
    P.on("vector", lambda v: v.tensor_mul(c_sco[:, :], c_sco[:, :], bc_inv[:, :]))
    shift_vvA = []
    for ct in range(NCT):
        shift_vvA.append(P.on("vector", lambda v, c2=ct: v.tensor_add(
            buf_h[:, c2 * 1024:(c2 + 1) * 1024],
            buf_h[:, c2 * 1024:(c2 + 1) * 1024], bc_negm[:, :]), inc=cV))
    P.wait("tensor", dcst, CST)

    rhs = lambda ct, th: buf_h[:, ct * 1024 + th * 512: ct * 1024 + (th + 1) * 512]
    kq_epi = rope_factory(
        c_cso, c_sco,
        lambda og, th: th * 512,
        lambda og, ot, th: kstage[:, ((2 * og + ot) % 8) * 1024 + th * 512:
                                  ((2 * og + ot) % 8) * 1024 + (th + 1) * 512])
    v_epi = v_factory()
    kv_written = [0]

    def q_pre_evac(og):
        if og >= 12:
            if og < 16:
                return (dqf[og % 4], 16)
            return (qds[og % 4], 16 * ((og - 12) // 4))
        return (dscr, dscr.n)

    q_epi = rope_factory(
        c_cso, c_sco, lambda og, th: th * 512,
        lambda og, ot, th: kstage[:, ((2 * (og - 8) + ot) % 8) * 1024 + th * 512:
                                  ((2 * (og - 8) + ot) % 8) * 1024 + (th + 1) * 512],
        pre_evac_wait=q_pre_evac)

    def a_epilogue(og, units, tv, defer):
        if og < 4:
            g = kq_epi(og, units, tv, defer)
            ctr, val = g

            def kdma(og2=og, val2=val):
                P.raw("scalar", lambda eng: eng.wait_ge(cS.sem, val2))
                for ot in range(2):
                    hh = 2 * og2 + ot
                    P.raw("scalar", lambda eng, h2=hh: eng.dma_start(
                        s_kl[h2 * 128:(h2 + 1) * 128, :],
                        kstage[:, h2 * 1024:(h2 + 1) * 1024]).then_inc(dscr.sem, 16))
            dscr.n += 32
            kv_written[0] += 2
            defer(kdma)
            return g
        if og < 8:
            return v_epi(og, units, tv, defer)
        if og == 8:
            kd_total = dscr.n
            P.wait("scalar", cS, cS.n)
            for tt in range(8):
                P.on("scalar", lambda sc, t2=tt: sc.dma_start(
                    s_vl[t2 * 128:(t2 + 1) * 128, :],
                    vstage[:, t2 * 1024:(t2 + 1) * 1024]), inc=dvscr)
            ag_state["vd"] = dvscr.n
            P.wait("gpsimd", dscr, kd_total)
            ag_state["v1"] = P.on("gpsimd", lambda gg: gg.collective_compute(
                "AllGather", mybir.AluOpType.bypass,
                replica_groups=[[0, 1, 2, 3], [4, 5, 6, 7]],
                ins=[s_kl.opt()], outs=[s_ka.opt()]), inc=dag)
        if og == 12:
            P.wait("gpsimd", dvscr, ag_state["vd"])
            P.wait("gpsimd", dag, ag_state["v1"])
            P.on("gpsimd", lambda gg: gg.collective_compute(
                "AllGather", mybir.AluOpType.bypass,
                replica_groups=[[0, 1, 2, 3], [4, 5, 6, 7]],
                ins=[s_vl.opt()], outs=[s_va.opt()]), inc=dag)
        g = q_epi(og, units, tv, defer)
        ctr, val = g
        qsem = dqf[og - 8] if og < 12 else qds[og % 4]
        qsem.n += 16

        def qdma(og2=og, val2=val, qs=qsem):
            P.raw("scalar", lambda eng: eng.wait_ge(cS.sem, val2))
            h0 = 2 * (og2 - 8)
            src = kstage[:, (h0 % 8) * 1024:(h0 % 8) * 1024 + 2048].rearrange(
                "p (g2 t) -> p g2 t", t=1024)
            dst = s_q[h0 * 128:(h0 + 2) * 128, :].rearrange("(g2 p) t -> p g2 t", p=128)
            P.raw("scalar", lambda eng, d=dst, s2=src: eng.dma_start(d, s2).then_inc(qs.sem, 16))

        defer(qdma)
        return g

    proj_stream(h_wqkv, 24, rhs, (c_bqkv, lambda og, ot: og * 256 + ot * 128,
                                  lambda t2: c_std[0:1, t2 * 512:(t2 + 1) * 512]),
                a_epilogue, th_count=2, psd_mode="quad",
                init_guards={0: (cS, bc2a), 1: (cS, bc2a)},
                rhs_ready=[(cV, v) for v in shift_vvA])
    QSCR = dscr.n
    QD_TOTALS = [q.n for q in qds]
    AG_DONE = dag.n

    c_entry = (cT.n, cTe.n, cS.n, cV.n, cG.n,
               [d.n for d in dfsp], [d.n for d in dfbc])
    head_done = None
    branch_q = {pp: {e: [] for e in ENGS} for pp in range(4)}
    ends = []
    mark6 = {}
    for pp in range(4):
        cT.n, cTe.n, cS.n, cV.n, cG.n = c_entry[:5]
        for d, v in zip(dfsp, c_entry[5]):
            d.n = v
        for d, v in zip(dfbc, c_entry[6]):
            d.n = v
        P.cur = branch_q[pp]
        hd_local = []
        nlo, nhi = 4 * (pp + 1), 4 * (8 - pp)
        prev_exp = {}
        prev_pv = {}
        prev_attn = {}
        prev_denred = {}
        prev_sfree = [None]
        prev_mult_guard = {0: None, 1: None}
        fin_cnt = {0: 0, 1: 0, 2: 0, 3: 0}
        P.wait("tensor", cS, c_entry[2])
        P.wait("tensor", cV, c_entry[3])
        for i in range(4):
            P.wait("vector", qds[i], QD_TOTALS[i])
            P.wait("vector", dqf[i], 16)
        for h8 in range(8):
            slot = h8 % 2
            P.wait("tensor", dkvs[slot], 17 * 16 * (h8 // 2 + 1))
            for piece in range(2):
                qoff = piece * 512
                nk = nlo if piece == 0 else nhi
                pv_pending = None

                def emit_pv(ktp, dmap, nk2, slot2):
                    for g in range(2):
                        for q2 in range(2):
                            qh = 2 * g + q2
                            dctr, dval = dmap[qh]
                            P.wait("tensor", dctr, dval)
                            if ktp == 0 and prev_attn.get(qh) is not None:
                                P.wait("tensor", cV, prev_attn[qh])
                            pcol = (2 * g + ktp % 2) * 1024 + q2 * 512
                            tvp = P.on("tensor", lambda t, s2=slot2, k2=ktp, g3=g, q3=q2, pc=pcol,
                                       f=(ktp == 0), l=(ktp == nk2 - 1):
                                       t.matmul(psd[2 + g3][:, q3 * 512:(q3 + 1) * 512],
                                                vbuf[:, s2 * 4096 + k2 * 128: s2 * 4096 + (k2 + 1) * 128],
                                                pbuf[:, pc:pc + 512], start=f, stop=l), inc=cT)
                            prev_pv[(g, ktp % 2)] = tvp

                last_den = {}
                for kt in range(nk):
                    par = kt % 2
                    sv_tv = {}
                    for g in range(2):
                        pe = prev_exp.get(g)
                        if pe is not None:
                            P.wait("tensor", cS, pe)
                        if kt == 0 and prev_sfree[0] is not None:
                            P.wait("tensor", cS, prev_sfree[0][g])
                        for q2 in range(2):
                            qh = 2 * g + q2
                            qcol = qh * 1024 + qoff
                            sv_tv[g] = P.on(
                                "tensor", lambda t, s2=slot, k2=kt, g3=g, q3=q2, qc=qcol:
                                t.matmul(psd[g3][:, q3 * 512:(q3 + 1) * 512],
                                         kbuf[:, s2 * 4096 + k2 * 128: s2 * 4096 + (k2 + 1) * 128],
                                         qbuf[:, s2 * 4096 + qc: s2 * 4096 + qc + 512],
                                         start=True, stop=True), inc=cT)
                    if pv_pending is not None:
                        emit_pv(pv_pending[0], pv_pending[1], nk, slot)
                    for g in range(2):
                        P.wait("scalar", cT, sv_tv[g])
                        pk = prev_pv.get((g, par))
                        if pk is not None:
                            P.wait("scalar", cT, pk)
                        prev_exp[g] = P.on(
                            "scalar", lambda sc, g3=g, p2=par:
                            sc.activation(pbuf[:, (2 * g3 + p2) * 1024:(2 * g3 + p2 + 1) * 1024],
                                          psd[g3][:, :], AF.Exp, scale=RSCALE), inc=cS)
                    den_ms = {}
                    for g in range(2):
                        gcol = (2 * g + par) * 1024
                        if kt == nk - 1 and g == 1:
                            dlt = 128 * (kt - (nk - 4))
                            for q2 in range(2):
                                eng = "gpsimd" if q2 == 0 else "vector"
                                ctr = cG if q2 == 0 else cV
                                P.wait(eng, cS, prev_exp[g])
                                if q2 == 1 and last_den.get(1) is not None:
                                    P.wait(eng, last_den[1][0], last_den[1][1])
                                P.on(eng, lambda v, pc=gcol + q2 * 512, d2=dlt:
                                     v.tensor_mul(pbuf[:, pc:pc + 512], pbuf[:, pc:pc + 512],
                                                  c_mext[:, 384 - d2:896 - d2]))
                                v0 = P.on(eng, lambda v, pc=gcol + q2 * 512, g3=g, q3=q2:
                                          v.tensor_add(
                                              dacc[:, g3 * 1024 + q3 * 512:g3 * 1024 + (q3 + 1) * 512],
                                              dacc[:, g3 * 1024 + q3 * 512:g3 * 1024 + (q3 + 1) * 512],
                                              pbuf[:, pc:pc + 512]), inc=ctr)
                                den_ms[2 * g + q2] = (ctr, v0)
                            continue
                        eng = "gpsimd" if g == 1 else "vector"
                        ctr = cG if g == 1 else cV
                        P.wait(eng, cS, prev_exp[g])
                        if kt >= nk - 4:
                            dlt = 128 * (kt - (nk - 4))
                            for q2 in range(2):
                                P.on(eng, lambda v, pc=gcol + q2 * 512, d2=dlt:
                                     v.tensor_mul(pbuf[:, pc:pc + 512], pbuf[:, pc:pc + 512],
                                                  c_mext[:, 384 - d2:896 - d2]))
                        if kt == 0:
                            for q2 in range(2):
                                pd0 = prev_denred.get(2 * g + q2)
                                if pd0 is not None:
                                    P.wait(eng, pd0[0], pd0[1])
                            v0 = P.on(eng, lambda v, pc=gcol, g3=g:
                                      v.tensor_copy(dacc[:, g3 * 1024:(g3 + 1) * 1024],
                                                    pbuf[:, pc:pc + 1024]), inc=ctr)
                        else:
                            v0 = P.on(eng, lambda v, pc=gcol, g3=g:
                                      v.tensor_add(dacc[:, g3 * 1024:(g3 + 1) * 1024],
                                                   dacc[:, g3 * 1024:(g3 + 1) * 1024],
                                                   pbuf[:, pc:pc + 1024]), inc=ctr)
                        den_ms[2 * g] = (ctr, v0)
                        den_ms[2 * g + 1] = (ctr, v0)
                        last_den[g] = (ctr, v0)
                    pv_pending = (kt, den_ms)
                emit_pv(pv_pending[0], pv_pending[1], nk, slot)
                den_fin = pv_pending[1]
                pv_pending = None
                tv1 = {}
                for qh in range(4):
                    P.wait("tensor", den_fin[qh][0], den_fin[qh][1])
                    tv1[qh] = P.on("tensor", lambda t, q4=qh: t.matmul(
                        psd[q4 // 2][0:1, (q4 % 2) * 512:(q4 % 2 + 1) * 512], c_ocol[:, :],
                        dacc[:, q4 * 512:(q4 + 1) * 512], start=True, stop=True), inc=cT)
                P.wait("scalar", cT, tv1[1])
                s0 = P.on("scalar", lambda sc: sc.activation(
                    dacc[0:1, 0:1024], psd[0][0:1, 0:1024], AF.Ln), inc=cS)
                P.wait("scalar", cT, tv1[3])
                s1 = P.on("scalar", lambda sc: sc.activation(
                    dacc[0:1, 1024:2048], psd[1][0:1, 0:1024], AF.Ln), inc=cS)
                vexp = P.on("scalar", lambda sc: sc.activation(
                    dacc[0:1, 0:2048], dacc[0:1, 0:2048], AF.Exp, scale=-1.0), inc=cS)
                prev_sfree[0] = {0: s0, 1: s1}
                va_of = {}
                for qh in range(4):
                    P.wait("scalar", cS, vexp)
                    if fin_cnt[qh] > 0:
                        P.wait("scalar", dfbc[qh], 16 * fin_cnt[qh])
                    P.on("scalar", lambda sc, q4=qh: sc.dma_start(
                        s_fin[q4:q4 + 1, :], dacc[0:1, q4 * 512:(q4 + 1) * 512]), inc=dfsp[qh])
                    prev_denred[qh] = (dfsp[qh], dfsp[qh].n)
                    if qh >= 2:
                        P.wait("gpsimd", cV, va_of[qh - 2])
                    elif prev_mult_guard[qh % 2] is not None:
                        P.wait("gpsimd", cV, prev_mult_guard[qh % 2])
                    P.wait("gpsimd", dfsp[qh], dfsp[qh].n)
                    P.on("gpsimd", lambda gg, q4=qh: gg.dma_start(
                        invden[:, (q4 % 2) * 512:(q4 % 2 + 1) * 512],
                        bass.AP(s_fin.tensor, q4 * 512, [[0, 128], [1, 512]])), inc=dfbc[qh])
                    fin_cnt[qh] += 1
                    P.wait("vector", dfbc[qh], dfbc[qh].n)
                    hq_g = 4 * h8 + qh
                    va = P.on("vector", lambda v, q4=qh, hg=hq_g, qo=qoff:
                              v.tensor_mul(buf_at[:, hg * 1024 + qo: hg * 1024 + qo + 512],
                                           psd[2 + q4 // 2][:, (q4 % 2) * 512:(q4 % 2 + 1) * 512],
                                           invden[:, (q4 % 2) * 512:(q4 % 2 + 1) * 512]), inc=cV)
                    prev_attn[qh] = va
                    va_of[qh] = va
                prev_mult_guard[0] = va_of[2]
                prev_mult_guard[1] = va_of[3]
            hd_local.append(prev_attn[3])
            if h8 == 6:
                mark6[pp] = cT.n
        ends.append((cT.n, cTe.n, cS.n, cV.n, cG.n, [d.n for d in dfsp], [d.n for d in dfbc]))
        if pp == 0:
            head_done = hd_local
        else:
            assert hd_local == head_done, f"branch {pp} head milestones differ"
    assert all(e == ends[0] for e in ends), f"branch ends differ: {ends}"
    cT.n, cTe.n, cS.n, cV.n, cG.n = ends[0][:5]
    for d, v in zip(dfsp, ends[0][5]):
        d.n = v
    for d, v in zip(dfbc, ends[0][6]):
        d.n = v
    P.cur = P.q

    def piece_owner(j):
        return j if j < 4 else 7 - j

    def piece_loff(j):
        return 0 if j < 4 else 512

    def kload(h8, slot):
        for j in range(8):
            r, lo = piece_owner(j), piece_loff(j)
            P.on("sync", lambda s, h2=h8, s2=slot, j2=j, r2=r, l2=lo: s.dma_start(
                kbuf[:, s2 * 4096 + j2 * 512: s2 * 4096 + (j2 + 1) * 512],
                s_ka[r2 * 1024 + h2 * 128: r2 * 1024 + (h2 + 1) * 128, l2:l2 + 512]),
                inc=dkvs[slot])

    def vload(h8, slot):
        for j in range(8):
            r, lo = piece_owner(j), piece_loff(j)
            P.on("sync", lambda s, h2=h8, s2=slot, j2=j, r2=r, l2=lo: s.dma_start(
                vbuf[:, s2 * 4096 + j2 * 512: s2 * 4096 + (j2 + 1) * 512].rearrange(
                    "p (k d2) -> p k d2", d2=128),
                s_va[r2 * 1024 + l2: r2 * 1024 + l2 + 512,
                     h2 * 128:(h2 + 1) * 128].rearrange("(k p) d2 -> p k d2", p=128)),
                inc=dkvs[slot])

    def qload(h8, slot):
        P.on("sync", lambda s, h2=h8, s2=slot: s.dma_start(
            qbuf[:, s2 * 4096:(s2 + 1) * 4096].rearrange("p (g t) -> p g t", t=1024),
            s_q[h2 * 512:(h2 + 1) * 512, :].rearrange("(g p) t -> p g t", p=128)), inc=dkvs[slot])

    P.wait("sync", dag, AG_DONE)
    vload(0, 0)
    vload(1, 1)
    for i in range(4):
        P.wait("sync", dqf[i], 16)
    qload(0, 0)
    qload(1, 1)
    P.wait("sync", cT, chunk_done[ABOUND - WRING + 3])
    kload(0, 0)
    P.wait("sync", cT, chunk_done[ABOUND - 1])
    kload(1, 1)
    P.wait("sync", dscr, QSCR)
    for i, qn in enumerate(QD_TOTALS):
        P.wait("sync", qds[i], qn)
    for h8 in range(2, 8):
        slot = h8 % 2
        P.wait("sync", cV, head_done[h8 - 2])
        kload(h8, slot)
        vload(h8, slot)
        qload(h8, slot)

    def mk_branch(e):
        bodies = [branch_q[pp][e] for pp in range(4)]

        def fn(eng, bodies=bodies, nm=e):
            with eng.register(f"br_{nm}") as r, eng.register(f"brm_{nm}") as r2:
                eng.reg_mov(r, eng.partition_id())
                eng.reg_mod(r2, r, 4)
                with eng.If_eq(r2, 0):
                    for f in bodies[0]:
                        f(eng)
                with eng.Else():
                    with eng.If_eq(r2, 1):
                        for f in bodies[1]:
                            f(eng)
                    with eng.Else():
                        with eng.If_eq(r2, 2):
                            for f in bodies[2]:
                                f(eng)
                        with eng.Else():
                            for f in bodies[3]:
                                f(eng)
        P.q[e].append(fn)

    for e in ("tensor", "scalar", "vector", "gpsimd"):
        mk_branch(e)

    assert all(mark6[pp] == mark6[0] for pp in range(4)), mark6
    P.wait("gpsimd", cV, cV.n)
    P.wait("gpsimd", cS, cS.n)
    P.wait("gpsimd", cT, mark6[0])
    P.on("gpsimd", lambda g: g.dma_start(c_bo[:, :], h_bo[:, :]), inc=dcst)
    CSTD = dcst.n
    ring_gates[_ci[0] + 4] = (cT, cT.n)
    P.wait("tensor", dcst, CSTD)
    P.wait("tensor", cS, cS.n)
    P.wait("tensor", cV, cV.n)
    P.wait("vector", cS, cS.n)
    P.wait("sync", cV, cV.n)
    stats_first = [True]
    ln2_last = [None]
    sq_done = {}

    def wo_epilogue(og, units, tv, defer):
        guard = None
        for ot in range(2):
            otg = 2 * og + ot
            xi, slot, xw = load_x(h_xown[otg * 128:(otg + 1) * 128, :])
            xb = otg % 2
            pd = psd[2 * (og % 2) + ot]
            P.wait("vector", xw[0], xw[1])
            P.wait("vector", cT, tv)
            if otg >= 2:
                P.wait("vector", dx1s[xb], 16 * (otg // 2))
                P.wait("vector", cS, sq_done[otg - 2])
            vv = P.on("vector", lambda v, sl=slot, x2=xb, u=pd:
                      v.tensor_add(x1buf[:, x2 * 1024:(x2 + 1) * 1024], xt(sl), u[:, :]), inc=cV)
            x_done[xi] = (cV, vv)
            P.wait("scalar", cV, vv)
            P.on("scalar", lambda sc, x2=xb, o2=otg: sc.dma_start(
                s_x1[o2 * 128:(o2 + 1) * 128, :], x1buf[:, x2 * 1024:(x2 + 1) * 1024]), inc=dx1s[xb])
            P.on("scalar", lambda sc, x2=xb, o2=otg: sc.copy(
                buf_h[:, o2 * 1024:(o2 + 1) * 1024], x1buf[:, x2 * 1024:(x2 + 1) * 1024]))
            vs, ss = stat_into(x1buf[:, xb * 1024:(xb + 1) * 1024], stats_first[0])
            stats_first[0] = False
            sq_done[otg] = ss
            ln2_last[0] = vs
            guard = (cV, vs)
        return guard

    rhso = lambda ct, th: buf_at[:, ct * 1024 + th * 512: ct * 1024 + (th + 1) * 512]
    proj_stream(h_wo, 16, rhso, (c_bo, lambda og, ot: og * 256 + ot * 128), wo_epilogue)
    X1_TOTALS = [d.n for d in dx1s]

    P.wait("gpsimd", cT, cT.n)
    P.wait("gpsimd", cV, cV.n)
    for dst, src in [(c_lb, h_lb), (c_lga, h_lga)]:
        P.on("gpsimd", lambda g, d=dst, s=src: g.dma_start(d[:, :], s[:, :]), inc=dcst)
    CSTE = dcst.n
    P.wait("tensor", dcst, CSTE)
    P.wait("tensor", cV, ln2_last[0])
    drain(ln_finalize(1024))
    bc2 = holder_fin[0]
    P.wait("sync", dx1s[0], X1_TOTALS[0])
    P.wait("sync", dx1s[1], X1_TOTALS[1])
    P.wait("vector", cS, bc2)
    shift_vv = []
    for ct in range(NCT):
        shift_vv.append(P.on("vector", lambda v, c2=ct: v.tensor_add(
            buf_h[:, c2 * 1024:(c2 + 1) * 1024],
            buf_h[:, c2 * 1024:(c2 + 1) * 1024], bc_negm[:, :]), inc=cV))
    P.wait("tensor", cS, bc2)

    tv = None
    for ct in range(NCT):
        P.wait("tensor", cV, shift_vv[ct])
        for th in range(2):
            tv = P.on("tensor", lambda t, c2=ct, t2=th: t.matmul(
                psd[2][0:36, t2 * 512:(t2 + 1) * 512], c_lga[:, c2 * 36:(c2 + 1) * 36],
                buf_h[:, c2 * 1024 + t2 * 512: c2 * 1024 + (t2 + 1) * 512],
                start=(c2 == 0), stop=(c2 == 31)), inc=cT)
    P.wait("vector", cT, tv)
    vv = P.on("vector", lambda v: v.tensor_mul(
        gtmp[32:36, :], psd[2][32:36, :], bc_inv[32:36, :]), inc=cV)
    P.wait("scalar", cV, vv)
    sv = P.on("scalar", lambda sc: sc.activation(
        lgrow[32:36, :], gtmp[32:36, :], AF.Identity, bias=c_ebc[32:36, 0:1]), inc=cS)
    P.on("scalar", lambda sc: sc.dma_start(grows[0:1, 0:1024], lgrow[32:33, :]), inc=dgr)
    P.on("scalar", lambda sc: sc.dma_start(grows[0:1, 1024:2048], lgrow[33:34, :]), inc=dgr)
    P.wait("sync", cS, sv)
    P.on("sync", lambda s: s.dma_start(grows[0:1, 2048:3072], lgrow[34:35, :]), inc=dgr)
    P.on("sync", lambda s: s.dma_start(grows[0:1, 3072:4096], lgrow[35:36, :]), inc=dgr)
    P.wait("vector", dgr, dgr.n)
    P.on("vector", lambda v: v.tensor_max(grows[0:1, 4096:5120], grows[0:1, 0:1024], grows[0:1, 1024:2048]))
    P.on("vector", lambda v: v.tensor_max(grows[0:1, 5120:6144], grows[0:1, 2048:3072], grows[0:1, 3072:4096]))
    P.on("vector", lambda v: v.tensor_max(grows[0:1, 6144:7168], grows[0:1, 4096:5120], grows[0:1, 5120:6144]))
    vv_ise = None
    for e in range(4):
        vv_ise = P.on("vector", lambda v, e2=e: v.tensor_tensor(
            ohr[0:1, e2 * 1024:(e2 + 1) * 1024], grows[0:1, e2 * 1024:(e2 + 1) * 1024],
            grows[0:1, 6144:7168], ALU.is_equal), inc=cV)

    zvh = [None]
    ig = {0: None, 1: None}

    def gate_bg(vise):
        P.wait("tensor", cV, vise)
        tvh = None
        for e in range(4):
            for t2 in range(2):
                tvh = P.on("tensor", lambda t, e2=e, t3=t2: t.matmul(
                    psd[3][0:32, t3 * 512:(t3 + 1) * 512], c_eall[0:1, 32 * e2:32 * (e2 + 1)],
                    ohr[0:1, e2 * 1024 + t3 * 512: e2 * 1024 + (t3 + 1) * 512],
                    start=(e2 == 0), stop=(e2 == 3)), inc=cT)
        yield
        P.wait("scalar", cT, tvh)
        sv2 = P.on("scalar", lambda sc: sc.copy(m32[0:32, :], psd[3][0:32, :]), inc=cS)
        yield
        P.wait("vector", cS, sv2)
        zvh[0] = P.on("vector", lambda v: v.tensor_mul(
            zrow[0:32, :], psd[2][0:32, :], m32[0:32, :]), inc=cV)
        ig[1] = (cV, zvh[0])
        yield

    first_delta = [True]

    def delta_acc(og, ot, th, unit):
        otg = 2 * og + ot
        if first_delta[0]:
            first_delta[0] = False
            P.wait("tensor", cV, zvh[0])
        return P.on("tensor", lambda t, o2=otg, t2=th, u=unit: t.matmul(
            u, c_lb[0:32, o2 * 128:(o2 + 1) * 128],
            zrow[0:32, t2 * 512:(t2 + 1) * 512], start=False, stop=True), inc=cT)

    def up_epilogue(og, units, tv, defer):
        guard = None
        for ot in range(2):
            otg = 2 * og + ot
            xi, slot, xw = load_x(s_x1[otg * 128:(otg + 1) * 128, :])
            xb = otg % 2
            pd = psd[2 * (og % 2) + ot]
            P.wait("vector", cT, tv)
            vv0 = P.on("vector", lambda v, u=pd: v.tensor_mul(
                htmp[:, :], u[:, :], bc_inv[:, :]), inc=cV)
            P.wait("vector", xw[0], xw[1])
            if otg >= 2:
                P.wait("vector", douts[xb], 16 * (otg // 2))
            vv = P.on("vector", lambda v, sl=slot, x2=xb:
                      v.tensor_add(outb[:, x2 * 1024:(x2 + 1) * 1024], xt(sl), htmp[:, :]), inc=cV)
            x_done[xi] = (cV, vv)
            P.wait("sync", cV, vv)
            P.on("sync", lambda s, x2=xb, o2=otg: s.dma_start(
                h_out[o2 * 128:(o2 + 1) * 128, :], outb[:, x2 * 1024:(x2 + 1) * 1024]), inc=douts[xb])
            guard = (cV, vv0)
        return guard

    rhsu = lambda ct, th: buf_h[:, ct * 1024 + th * 512: ct * 1024 + (th + 1) * 512]
    proj_stream(h_wup, 16, rhsu, None, up_epilogue, extra_acc=delta_acc,
                bg=gate_bg(vv_ise), init_guards=ig)

    P.wait("sync", douts[0], douts[0].n)
    P.wait("sync", douts[1], douts[1].n)

    block = ctx.enter_context(nc.Block())
    for e in ENGS:
        def make(fn_list):
            def run(eng):
                for f in fn_list:
                    f(eng)
            return run
        getattr(block, e)(make(P.q[e]))
    ctx.close()
    return nc


def kernel(**inputs):
    per_core = _host_prep(inputs)
    nc = build_program()
    res = run_bass_kernel_spmd(nc, per_core, core_ids=list(range(8)))
    out = np.empty((NB, SEQ, D), dtype=np.float32)
    for c in range(8):
        b, p = c // 4, c % 4
        o = res.results[c]["outT"]
        out[b, 512 * p:512 * p + 512, :] = o[:, 0:512].T
        out[b, 512 * (7 - p):512 * (7 - p) + 512, :] = o[:, 512:1024].T
    return out


# version 18
# speedup vs baseline: 1.0125x; 1.0125x over previous
import numpy as np
import ml_dtypes
import concourse.bass as bass
import concourse.mybir as mybir
from concourse.bass_utils import run_bass_kernel_spmd

F32 = mybir.dt.float32
BF16 = mybir.dt.bfloat16
AF = mybir.ActivationFunctionType
ALU = mybir.AluOpType

D = 4096
SEQ = 4096
NB = 2
NH = 32
NKV = 8
HD = 128
TOK = 1024
NCT = D // 128
BLK = 1024
NBLK = SEQ // BLK
RSCALE = float(1.0 / np.sqrt(HD))
EPS = 1e-5
ENGS = ("sync", "tensor", "scalar", "vector", "gpsimd")
WRING = 8
WRING_DE = 6
ABOUND = 192
XRING = 4



def _perm_rows(n_heads):
    p = np.empty(128, dtype=np.int64)
    p[:64] = np.arange(64) * 2
    p[64:] = np.arange(64) * 2 + 1
    return (np.arange(n_heads)[:, None] * 128 + p[None, :]).reshape(-1)


def _stream(wT):
    c, O = wT.shape
    nog = O // 256
    a = wT.reshape(NCT, 128, nog, 256).transpose(2, 0, 1, 3)
    a = a.reshape(nog, 8, 4, 128, 256).reshape(nog * 8, 4, 128, 256)
    return np.ascontiguousarray(a)


def _host_prep(inputs):
    x = np.asarray(inputs["L0_attn_x"], dtype=np.float32)
    wqkv = np.asarray(inputs["L0_attn_w_qkv"], dtype=np.float32)
    bqkv = np.asarray(inputs["L0_attn_b_qkv"], dtype=np.float32)
    wo = np.asarray(inputs["L0_attn_w_o"], dtype=np.float32)
    bo = np.asarray(inputs["L0_attn_b_o"], dtype=np.float32)
    wup = np.asarray(inputs["L0_ffn_W_up"], dtype=np.float32)
    gw = np.asarray(inputs["L0_ffn_gate_w"], dtype=np.float32)
    eb = np.asarray(inputs["L0_ffn_expert_bias"], dtype=np.float32)
    la = np.asarray(inputs["L0_ffn_lora_A"], dtype=np.float32)
    lb = np.asarray(inputs["L0_ffn_lora_B"], dtype=np.float32)

    pq = _perm_rows(NH)
    pk = _perm_rows(NKV)
    wq = wqkv[0:4096][pq]
    wkvq = np.concatenate([wqkv[4096:5120][pk], wqkv[5120:6144], wq], axis=0)

    shared = {
        "wqkv_s": _stream(np.ascontiguousarray(wkvq.T)),
        "wo_s": _stream(np.ascontiguousarray(wo.T)),
        "wup_s": _stream(np.ascontiguousarray(wup.T)),
        "bqkv_row": np.ascontiguousarray(
            np.concatenate([bqkv[4096:5120][pk], bqkv[5120:6144], bqkv[0:4096][pq]]))[None, :],
        "bo_row": np.ascontiguousarray(bo)[None, :],
        "eb_row": np.ascontiguousarray(eb)[None, :],
        "eb_col": np.concatenate([np.zeros((32, 1), np.float32),
                                  np.ascontiguousarray(eb)[:, None],
                                  np.zeros((28, 1), np.float32)]),
        "lga_re": np.ascontiguousarray(np.concatenate([
            la.reshape(32, D).T.reshape(NCT, 128, 32).transpose(1, 0, 2),
            gw.T.reshape(NCT, 128, 4).transpose(1, 0, 2)], axis=2).reshape(128, NCT * 36)),
        "lb_s": np.ascontiguousarray(lb.transpose(0, 2, 1).reshape(32, D)),
    }

    pos = np.arange(SEQ, dtype=np.float64)
    invf = 1.0 / (10000.0 ** (np.arange(0, HD, 2, dtype=np.float64) / HD))
    ang = invf[:, None] * pos[None, :]
    cosj = np.cos(ang).astype(np.float32)
    sinj = np.sin(ang).astype(np.float32)
    cs_full = np.ascontiguousarray(np.concatenate([cosj, sinj], axis=0))
    sc_full = np.ascontiguousarray(np.concatenate([sinj, cosj], axis=0))
    A = np.zeros((128, 128), dtype=np.float32)
    Bm = np.zeros((128, 128), dtype=np.float32)
    j = np.arange(64)
    A[j, j] = 1.0
    A[j, 64 + j] = -1.0
    Bm[64 + j, j] = 1.0
    Bm[64 + j, 64 + j] = 1.0
    ab = np.ascontiguousarray(np.concatenate([A.T, Bm.T], axis=1))
    mext = (np.arange(896)[None, :] - 384 >= np.arange(128)[:, None]).astype(np.float32)
    e_all = np.zeros((1, 128), dtype=np.float32)
    for e in range(4):
        e_all[0, 32 * e + 8 * e: 32 * e + 8 * e + 8] = 1.0
    shared.update({
        "cs_full": cs_full, "sc_full": sc_full, "ab": ab,
        "mext": np.ascontiguousarray(mext),
        "e_all": e_all, "id128": np.eye(128, dtype=np.float32),
        "ones_row": np.ones((1, 1024), dtype=np.float32),
        "ones128_row": np.ones((1, 128), dtype=np.float32),
        "ones_col": np.ones((128, 1), dtype=np.float32),
        "eps_c": np.full((1, 1), 1e-5, dtype=np.float32),
    })

    per_core = []
    xT = [np.ascontiguousarray(x[b].T) for b in range(NB)]
    for c in range(8):
        b, p = c // 4, c % 4
        cols = np.concatenate([np.arange(512 * p, 512 * p + 512),
                               np.arange(512 * (7 - p), 512 * (7 - p) + 512)])
        m = dict(shared)
        m["xT"] = xT[b]
        m["x_ownT"] = np.ascontiguousarray(xT[b][:, cols])
        m["cs_own"] = np.ascontiguousarray(cs_full[:, cols])
        m["sc_own"] = np.ascontiguousarray(sc_full[:, cols])
        per_core.append(m)
    return per_core



class Ctr:
    def __init__(self, sem, step=1):
        self.sem, self.step, self.n = sem, step, 0


class Plan:

    def __init__(self):
        self.q = {e: [] for e in ENGS}
        self.cur = self.q
        self.autos = {}

    def on(self, e, fn, inc=None):
        auto = self.autos.get(e)
        if auto is not None and (inc is None or inc is auto):
            if auto.n > 0:
                self.cur[e].append(lambda eng, s2=auto.sem, v2=auto.n: eng.wait_ge(s2, v2))
            auto.n += 1
            v = auto.n
            self.cur[e].append(lambda eng, s2=auto.sem: fn(eng).then_inc(s2, 1))
            return v
        if auto is not None and auto.n > 0:
            self.cur[e].append(lambda eng, s2=auto.sem, v2=auto.n: eng.wait_ge(s2, v2))
        if inc is None:
            self.cur[e].append(fn)
            return None
        inc.n += inc.step
        v = inc.n
        sem, st = inc.sem, inc.step
        self.cur[e].append(lambda eng: fn(eng).then_inc(sem, st))
        return v

    def raw(self, e, fn):
        self.cur[e].append(fn)

    def wait(self, e, ctr, val):
        if val is None or val <= 0:
            return
        self.cur[e].append(lambda eng, s=ctr.sem, v=val: eng.wait_ge(s, v))


def build_program():
    nc = bass.Bass("TRN2", target_bir_lowering=False, num_swdge_queues=2)

    def din(name, shape):
        return nc.dram_tensor(name, list(shape), F32, kind="ExternalInput").ap()

    h_xT = din("xT", (D, SEQ))
    h_xown = din("x_ownT", (D, TOK))
    h_wqkv = din("wqkv_s", (192, 4, 128, 256))
    h_wo = din("wo_s", (128, 4, 128, 256))
    h_wup = din("wup_s", (128, 4, 128, 256))
    h_bqkv = din("bqkv_row", (1, 6144))
    h_bo = din("bo_row", (1, 4096))
    h_eb = din("eb_row", (1, 4))
    h_ebc = din("eb_col", (64, 1))
    h_lga = din("lga_re", (128, NCT * 36))
    h_lb = din("lb_s", (32, D))
    h_csf = din("cs_full", (128, SEQ))
    h_scf = din("sc_full", (128, SEQ))
    h_cso = din("cs_own", (128, TOK))
    h_sco = din("sc_own", (128, TOK))
    h_ab = din("ab", (128, 256))
    h_mext = din("mext", (128, 896))
    h_eall = din("e_all", (1, 128))
    h_id = din("id128", (128, 128))
    h_ones = din("ones_row", (1, 1024))
    h_ones128 = din("ones128_row", (1, 128))
    h_onescol = din("ones_col", (128, 1))
    h_eps = din("eps_c", (1, 1))
    h_out = nc.dram_tensor("outT", [D, TOK], F32, kind="ExternalOutput").ap()

    s_kl = nc.dram_tensor("k_loc", [NKV * 128, TOK], BF16, kind="Internal").ap()
    s_vl = nc.dram_tensor("v_loc", [TOK, NKV * 128], BF16, kind="Internal").ap()
    s_ka = nc.dram_tensor("k_all", [4 * NKV * 128, TOK], BF16, kind="Internal").ap()
    s_va = nc.dram_tensor("v_all", [4 * TOK, NKV * 128], BF16, kind="Internal").ap()
    s_q = nc.dram_tensor("q_scr", [NH * 128, TOK], BF16, kind="Internal").ap()
    s_x1 = nc.dram_tensor("x1_scr", [D, TOK], F32, kind="Internal").ap()
    s_fin = nc.dram_tensor("fin_scr", [4, 512], F32, kind="Internal").ap()

    from contextlib import ExitStack
    ctx = ExitStack()

    ARENA = 204 * 1024
    arena = nc.alloc_sbuf_tensor("arena", [128, ARENA], mybir.dt.uint8)
    BASE = nc.lookup_mloc(arena).addr

    def place(name, shape, dtype, off_kb):
        off = BASE + int(round(off_kb * 1024))
        return nc.alloc_sbuf_tensor_at(name, list(shape), dtype, offset=off).ap()

    buf_h = place("buf_h", [128, NCT * 1024], BF16, 0)
    kstage = place("kstage", [128, NKV * 1024], BF16, 64)
    vstage = place("vstage", [128, 8 * 1024], BF16, 80)
    vtmp = place("vtmp", [128, 2 * 512], BF16, 96)
    c_bqkv = place("c_bqkv", [1, 6144], BF16, 98)
    c_cso = place("c_cso", [128, TOK], BF16, 110)
    c_sco = place("c_sco", [128, TOK], BF16, 112)
    tab = place("tab", [128, 4 * 512], BF16, 118)
    c_ab = place("c_ab", [128, 256], BF16, 122)
    c_id = place("c_id", [128, 128], BF16, 123)
    buf_at = place("buf_at", [128, NCT * 1024], BF16, 64)
    c_lb = place("c_lb", [32, D], BF16, 64)
    c_lga = place("c_lga", [128, NCT * 36], BF16, 72)
    zrow = place("zrow", [32, 1024], BF16, 74.75)
    m32 = place("m32", [32, 1024], BF16, 76.75)
    lgrow = place("lgrow", [64, 1024], F32, 78.75)
    ohr = place("ohr", [1, 4096], BF16, 82.75)
    grows = place("grows", [1, 7168], F32, 90.75)
    wring = place("wring", [128, WRING * 1024], BF16, 128)
    xroll = place("xroll", [128, XRING * 1024], F32, 140)
    x1buf = place("x1buf", [128, 2 * 1024], F32, 156)
    outb = place("outb", [128, 2 * 1024], F32, 156)
    kbuf = place("kbuf", [128, 2 * 4096], BF16, 128)
    vbuf = place("vbuf", [128, 2 * 4096], BF16, 144)
    qbuf = place("qbuf", [128, 2 * 4096], BF16, 160)
    sacc = place("sacc", [128, 1024], F32, 164)
    qacc = place("qacc", [128, 1024], F32, 168)
    xsq = place("xsq", [128, 1024], F32, 172)
    dacc = place("dacc", [128, 4 * 512], F32, 176)
    invden = place("invden", [128, 2 * 512], F32, 184)
    xsq2 = place("xsq2", [128, 1024], F32, 184)
    gtmp = place("gtmp", [128, 1024], F32, 188)
    pbuf = place("pbuf", [128, 4 * 1024], BF16, 188)
    bc_negm = place("bc_negm", [128, 1024], F32, 176)
    bc_inv = place("bc_inv", [128, 1024], F32, 180)
    c_bo = place("c_bo", [1, 4096], BF16, 192)
    rows = place("rows", [1, 2048], F32, 192)
    c_mext = place("c_mext", [128, 896], BF16, 200)
    c_std = place("c_std", [1, 1024], BF16, 114)
    c_ocolb = place("c_ocolb", [128, 1], BF16, 116)
    xsqA = place("xsqA", [128, 1024], BF16, 172)
    xsqB = place("xsqB", [128, 1024], BF16, 184)
    c_ones = place("c_ones", [1, 512], BF16, 202)
    c_o128f = place("c_o128f", [1, 128], F32, 203)
    c_ocol = place("c_ocol", [128, 1], F32, 203.5)
    c_eb = place("c_eb", [1, 4], BF16, 203.625)
    c_ebc = place("c_ebc", [64, 1], F32, 203.53125)
    c_eall = place("c_eall", [1, 128], BF16, 203.75)
    c_eps = place("c_eps", [1, 1], F32, 203.6875)
    htmp = xsq

    psd = [ctx.enter_context(nc.psum_tensor(f"psd{i}", [128, 1024], F32)) for i in range(4)]

    mksem = lambda n: ctx.enter_context(nc.semaphore(n))
    dcst = Ctr(mksem("dcst"), 16)
    dws = [Ctr(mksem(f"dw{i}"), 16) for i in range(WRING)]
    dxs = [Ctr(mksem(f"dx{i}"), 16) for i in range(XRING)]
    dscr = Ctr(mksem("dscr"), 16)
    dvscr = Ctr(mksem("dvscr"), 16)
    dkvs = [Ctr(mksem(f"dkv{i}"), 16) for i in range(2)]
    qds = [Ctr(mksem(f"qd{i}"), 16) for i in range(4)]
    dqf = [Ctr(mksem(f"dqf{i}"), 16) for i in range(4)]
    dx1s = [Ctr(mksem(f"dx1{i}"), 16) for i in range(2)]
    douts = [Ctr(mksem(f"dout{i}"), 16) for i in range(2)]
    dgr = Ctr(mksem("dgr"), 16)
    dfsp = [Ctr(mksem(f"dfsp{i}"), 16) for i in range(4)]
    dfbc = [Ctr(mksem(f"dfbc{i}"), 16) for i in range(4)]
    cG = Ctr(mksem("cG"))
    cT = Ctr(mksem("cT"))
    cTe = Ctr(mksem("cTe"))
    cVe = Ctr(mksem("cVe"))
    cS = Ctr(mksem("cS"))
    cV = Ctr(mksem("cV"))

    P = Plan()
    P.autos = {"vector": cV, "scalar": cS, "gpsimd": cG}

    docb = Ctr(mksem("docb"), 16)
    P.on("gpsimd", lambda g: g.dma_start(c_ocolb[:, :], h_onescol[:, :]), inc=docb)
    for dst, src in [(c_cso, h_cso[:, :]), (c_sco, h_sco[:, :]), (c_ab, h_ab[:, :]),
                     (c_id, h_id[:, :]), (c_mext, h_mext[:, :]), (c_eall, h_eall[:, :]),
                     (c_ones, h_ones[0:1, 0:512]), (c_o128f, h_ones128[:, :]),
                     (c_ocol, h_onescol[:, :]), (c_bqkv, h_bqkv[:, :]), (c_eb, h_eb[:, :]),
                     (c_ebc, h_ebc[:, :]), (c_eps, h_eps[:, :])]:
        P.on("gpsimd", lambda g, d=dst, s=src: g.dma_start(d[:, :], s), inc=dcst)
    CST = dcst.n
    P.wait("tensor", docb, 16)

    chunk_done = {}
    _ci = [0]

    ring_gates = {}
    pre_loaded = {}

    def load_chunk(src_h, li):
        gi = _ci[0]
        _ci[0] += 1
        if gi < ABOUND:
            slot = gi % WRING
            prev_gi = gi - WRING
        else:
            li2 = gi - ABOUND
            slot = li2 % WRING_DE
            prev_gi = gi - WRING_DE if li2 >= WRING_DE else ABOUND - WRING + li2
        rg = ring_gates.pop(gi, None)
        if rg is not None:
            P.wait("gpsimd", rg[0], rg[1])
        if prev_gi >= 0:
            P.wait("gpsimd", cT, chunk_done[prev_gi])
        dst = wring[:, slot * 1024:(slot + 1) * 1024].rearrange("p (j c) -> p j c", c=256)
        src = src_h[li, :, :, :].rearrange("j p c -> p j c")
        wt = P.on("gpsimd", lambda g, d=dst, s=src: g.dma_start(d, s), inc=dws[slot])
        return gi, slot, (dws[slot], wt)

    def wslice(slot, j, ot):
        o = slot * 1024 + j * 256 + ot * 128
        return wring[:, o:o + 128]

    x_done = {}
    _xi = [0]

    def load_x(src_ap, width=1024):
        xi = _xi[0]
        _xi[0] += 1
        slot = xi % XRING
        if xi - XRING >= 0:
            ctr, val = x_done[xi - XRING]
            P.wait("sync", ctr, val)
        wt = P.on("sync", lambda s, d=xroll[:, slot * 1024:slot * 1024 + width], a=src_ap:
                  s.dma_start(d, a), inc=dxs[slot])
        return xi, slot, (dxs[slot], wt)

    def xtw(slot, width):
        return xroll[:, slot * 1024:slot * 1024 + width]

    def xt(slot):
        return xroll[:, slot * 1024:(slot + 1) * 1024]

    holder_stats = [None]
    xsq_user = {0: None, 1: None}
    xsqb = {0: xsq, 1: xsq2}

    def ln_stats(src_fn, width=1024):
        last = None
        for ct in range(NCT):
            par = ct % 2
            xb = xsqb[par]
            xi, slot, xw = load_x(src_fn(ct), width)
            P.wait("vector", xw[0], xw[1])
            if ct == 0:
                P.on("vector", lambda v, sl=slot, w=width: v.tensor_copy(sacc[:, 0:w], xtw(sl, w)))
            else:
                P.on("vector", lambda v, sl=slot, w=width: v.tensor_add(sacc[:, 0:w], sacc[:, 0:w], xtw(sl, w)))
            P.wait("scalar", xw[0], xw[1])
            if xsq_user[par] is not None:
                P.wait("scalar", cV, xsq_user[par])
            sv = P.on("scalar", lambda sc, sl=slot, w=width, x2=xb: sc.activation(x2[:, 0:w], xtw(sl, w), AF.Square), inc=cS)
            P.wait("vector", cS, sv)
            if ct == 0:
                vv = P.on("vector", lambda v, w=width, x2=xb: v.tensor_copy(qacc[:, 0:w], x2[:, 0:w]), inc=cV)
            else:
                vv = P.on("vector", lambda v, w=width, x2=xb: v.tensor_add(qacc[:, 0:w], qacc[:, 0:w], x2[:, 0:w]), inc=cV)
            xsq_user[par] = vv
            x_done[xi] = (cV, vv)
            last = vv
            yield
        holder_stats[0] = last

    stat_par = [0]

    def stat_into(tile_ap, first):
        par = stat_par[0]
        stat_par[0] ^= 1
        xb = xsqb[par]
        if first:
            P.on("vector", lambda v: v.tensor_copy(sacc[:, :], tile_ap))
        else:
            P.on("vector", lambda v: v.tensor_add(sacc[:, :], sacc[:, :], tile_ap))
        if xsq_user[par] is not None:
            P.wait("scalar", cV, xsq_user[par])
        sv = P.on("scalar", lambda sc, x2=xb: sc.activation(x2[:, :], tile_ap, AF.Square), inc=cS)
        P.wait("vector", cS, sv)
        if first:
            vv = P.on("vector", lambda v, x2=xb: v.tensor_copy(qacc[:, :], x2[:, :]), inc=cV)
        else:
            vv = P.on("vector", lambda v, x2=xb: v.tensor_add(qacc[:, :], qacc[:, :], x2[:, :]), inc=cV)
        xsq_user[par] = vv
        holder_stats[0] = vv
        return vv, sv

    def ln_finalize(width=1024, psum_tv=None, mkstd=False):
        nh = width // 512
        if psum_tv is None:
            P.wait("tensor", cV, holder_stats[0])
            tv = None
            for hh in range(nh):
                P.on("tensor", lambda t, h2=hh: t.matmul(
                    psd[2][0:1, h2 * 512:(h2 + 1) * 512], c_ocol[:, :],
                    sacc[:, h2 * 512:(h2 + 1) * 512], start=True, stop=True))
                tv = P.on("tensor", lambda t, h2=hh: t.matmul(
                    psd[3][0:1, h2 * 512:(h2 + 1) * 512], c_ocol[:, :],
                    qacc[:, h2 * 512:(h2 + 1) * 512], start=True, stop=True), inc=cT)
            yield
        else:
            tv = psum_tv
        P.wait("scalar", cT, tv)
        P.on("scalar", lambda sc, w=width: sc.activation(rows[0:1, 0:w], psd[2][0:1, 0:w], AF.Copy, scale=1.0 / D))
        sv = P.on("scalar", lambda sc, w=width: sc.activation(rows[0:1, 1024:1024 + w], psd[3][0:1, 0:w], AF.Copy, scale=1.0 / D), inc=cS)
        P.wait("vector", cS, sv)
        P.on("vector", lambda v, w=width: v.tensor_mul(bc_negm[0:1, 0:w], rows[0:1, 0:w], rows[0:1, 0:w]))
        vv = P.on("vector", lambda v, w=width: v.tensor_sub(
            bc_inv[0:1, 0:w], rows[0:1, 1024:1024 + w], bc_negm[0:1, 0:w]), inc=cV)
        P.wait("scalar", cV, vv)
        P.on("scalar", lambda sc, w=width: sc.activation(bc_negm[0:1, 0:w], bc_inv[0:1, 0:w], AF.Ln, bias=c_eps[0:1, 0:1]))
        P.on("scalar", lambda sc, w=width: sc.activation(
            rows[0:1, 1024:1024 + w], bc_negm[0:1, 0:w], AF.Exp, scale=-0.5))
        if mkstd:
            P.on("scalar", lambda sc, w=width: sc.activation(
                c_std[0:1, 0:w], bc_negm[0:1, 0:w], AF.Exp, scale=0.5))
        sv = P.on("scalar", lambda sc, w=width: sc.activation(bc_inv[0:1, 0:w], rows[0:1, 0:w], AF.Copy, scale=-1.0), inc=cS)
        yield
        P.wait("tensor", cS, sv)
        for hh in range(nh):
            P.on("tensor", lambda t, h2=hh: t.matmul(
                psd[2][:, h2 * 512:(h2 + 1) * 512], c_o128f[:, :],
                bc_inv[0:1, h2 * 512:(h2 + 1) * 512], start=True, stop=True))
            tv = P.on("tensor", lambda t, h2=hh: t.matmul(
                psd[3][:, h2 * 512:(h2 + 1) * 512], c_o128f[:, :],
                rows[0:1, 1024 + h2 * 512:1024 + (h2 + 1) * 512], start=True, stop=True), inc=cT)
        yield
        P.wait("scalar", cT, tv)
        P.on("scalar", lambda sc, w=width: sc.copy(bc_negm[:, 0:w], psd[2][:, 0:w]))
        sv = P.on("scalar", lambda sc, w=width: sc.copy(bc_inv[:, 0:w], psd[3][:, 0:w]), inc=cS)
        holder_fin[0] = sv
        yield

    holder_fin = [None]

    def h_pass(src_fn, dst_fn, bc_ready, width=1024, te_guard=None, bc_off=0):
        P.wait("vector", cS, bc_ready)
        P.wait("gpsimd", cS, bc_ready)
        if te_guard is not None:
            P.wait("vector", cT, te_guard)
            P.wait("gpsimd", cT, te_guard)
        lastv, lastg = None, None
        for ct in range(NCT):
            xi, slot, xw = load_x(src_fn(ct), width)
            eng = "gpsimd" if ct % 3 == 2 else "vector"
            ctr = cG if eng == "gpsimd" else cV
            tb = gtmp if eng == "gpsimd" else htmp
            P.wait(eng, xw[0], xw[1])
            P.on(eng, lambda v, sl=slot, w=width, o=bc_off, t2=tb:
                 v.tensor_add(t2[:, 0:w], xtw(sl, w), bc_negm[:, o:o + w]))
            vv = P.on(eng, lambda v, c2=ct, w=width, o=bc_off, t2=tb:
                      v.tensor_mul(dst_fn(c2), t2[:, 0:w], bc_inv[:, o:o + w]), inc=ctr)
            if eng == "vector":
                xsq_user[0] = vv
                lastv = vv
            else:
                lastg = vv
            x_done[xi] = (ctr, vv)
            yield
        holder_h[0] = lastv
        holder_h[1] = lastg
        yield

    holder_h = [None, None]

    def ln_chain(src_fn, dst_fn, width=1024, te_guard=None):
        for y in ln_stats(src_fn, width):
            yield
        for y in ln_finalize(width):
            yield
        bc = holder_fin[0]
        for y in h_pass(src_fn, dst_fn, bc, width, te_guard):
            yield

    def drain(gen):
        if gen is None:
            return
        for _ in gen:
            pass

    def proj_stream(src_h, nog, rhs_fn, bias, epilogue, extra_acc=None,
                    th_count=2, bg=None, init_guards=None, psd_mode="quad",
                    rhs_ready=None):
        pend = []

        def defer(fn):
            pend.append(fn)

        def flush():
            nonlocal pend
            for fn in pend:
                fn()
            pend = []

        def bg_step(k):
            nonlocal bg
            if bg is None:
                return
            try:
                for _ in range(k):
                    next(bg)
            except StopIteration:
                bg = None

        guards = init_guards if init_guards is not None else {0: None, 1: None}
        for og in range(nog):
            pp = og % 2 if psd_mode == "quad" else 0
            if psd_mode == "quad":
                if th_count == 2:
                    units = {(ot, th): psd[2 * pp + ot][:, th * 512:(th + 1) * 512]
                             for ot in range(2) for th in range(2)}
                else:
                    units = {(ot, 0): psd[pp][:, ot * 512:(ot + 1) * 512] for ot in range(2)}
            else:
                units = {(ot, th): psd[ot][:, th * 512:(th + 1) * 512]
                         for ot in range(2) for th in range(th_count)}

            def start_group():
                if guards[pp] is not None:
                    ctr, val = guards[pp]
                    P.wait("tensor", ctr, val)
                if bias is not None:
                    rowh, coloff = bias[0], bias[1]
                    brhs = bias[2] if len(bias) > 2 else (
                        lambda t2: c_ones[0:1, 0:512])
                    for (ot, th), u in units.items():
                        P.on("tensor", lambda t, u2=u, rh=rowh, t3=th,
                             co=coloff(og, ot): t.matmul(
                                 u2, rh[0:1, co:co + 128], brhs(t3),
                                 start=True, stop=False))

            delayed = (psd_mode == "compact")
            if not delayed:
                start_group()
            stash = []
            tv = None
            nu = len(units)
            for sci in range(8):
                gi, slot, cwt = load_chunk(src_h, og * 8 + sci)
                if sci == 2:
                    flush()
                    if delayed:
                        start_group()
                        for fn in stash:
                            fn()
                        stash = []
                if sci >= 3:
                    bg_step(3)

                def emit_chunk(sci2=sci, gi2=gi, slot2=slot, cwt2=cwt):
                    nonlocal tv
                    if rhs_ready is not None and og == 0:
                        ctr2, val2 = rhs_ready[sci2 * 4 + 3]
                        P.wait("tensor", ctr2, val2)
                    P.wait("tensor", cwt2[0], cwt2[1])
                    for jj in range(4):
                        ct = sci2 * 4 + jj
                        for ui, ((ot, th), u) in enumerate(units.items()):
                            first = (ct == 0 and bias is None)
                            last = (ct == 31 and extra_acc is None)
                            is_tail = (jj == 3 and ui == nu - 1)
                            tv = P.on("tensor", lambda t, sl=slot2, j2=jj, o2=ot, t2=th,
                                      f=first, l=last, c2=ct, u2=u:
                                      t.matmul(u2, wslice(sl, j2, o2), rhs_fn(c2, t2),
                                               start=f, stop=l),
                                      inc=cT if is_tail else None)
                    chunk_done[gi2] = tv

                if delayed and sci < 2:
                    stash.append(emit_chunk)
                else:
                    emit_chunk()
            if extra_acc is not None:
                for (ot, th), u in units.items():
                    tv = extra_acc(og, ot, th, u)
            guards[pp] = epilogue(og, units, tv, defer)
        flush()
        bg_step(1000000)
        return guards

    def rope_factory(cs_const, sc_const, tokoff_fn, stage_dst_fn, pre_evac_wait=None):
        state = {"u": 0, "prev": None}

        def epi(og, units, tv, defer):
            last_guard = None
            for (ot, th), u in units.items():
                up = state["u"] % 2
                ta = tab[:, up * 1024: up * 1024 + 512]
                tb = tab[:, up * 1024 + 512: up * 1024 + 1024]
                to = tokoff_fn(og, th)
                P.wait("vector", cT, tv)
                if state["prev"] is not None:
                    P.wait("vector", cTe, state["prev"])
                P.on("vector", lambda v, u2=u, a=ta, o=to:
                     v.tensor_mul(a, u2, cs_const[:, o:o + 512]))
                vv = P.on("vector", lambda v, u2=u, b2=tb, o=to:
                          v.tensor_mul(b2, u2, sc_const[:, o:o + 512]), inc=cV)
                cTe.n += 2
                trv = cTe.n
                state["prev"] = trv
                cS.n += 1
                sv = cS.n
                dst = stage_dst_fn(og, ot, th)
                sguard = pre_evac_wait(og) if pre_evac_wait else None

                def te_part(u2=u, a=ta, b2=tb, vv2=vv, trv2=trv, sv2=sv, d=dst, sg=sguard):
                    P.raw("tensor", lambda eng: eng.wait_ge(cV.sem, vv2))
                    P.raw("tensor", lambda eng: eng.matmul(
                        u2, c_ab[:, 0:128], a, start=True, stop=False).then_inc(cTe.sem, 1))
                    P.raw("tensor", lambda eng: eng.matmul(
                        u2, c_ab[:, 128:256], b2, start=False, stop=True).then_inc(cTe.sem, 1))
                    P.raw("scalar", lambda eng: eng.wait_ge(cTe.sem, trv2))
                    if sg is not None:
                        P.raw("scalar", lambda eng, s2=sg: eng.wait_ge(s2[0].sem, s2[1]))
                    P.raw("scalar", lambda eng: eng.wait_ge(cS.sem, sv2 - 1))
                    P.raw("scalar", lambda eng: eng.copy(d, u2).then_inc(cS.sem, 1))

                defer(te_part)
                state["u"] += 1
                last_guard = (cS, sv)
            return last_guard

        return epi

    def v_factory():
        state = {"u": 0, "prev": None}
        vview_all = vstage[:, :].rearrange("p (j c) -> p j c", c=1024)

        def epi(og, units, tv, defer):
            last_guard = None
            for (ot, th), u in units.items():
                up = state["u"] % 2
                vt = vtmp[:, up * 512:(up + 1) * 512]
                vo = (og - 4) * 256 + ot * 128
                prev = state["prev"]
                cVe.n += 1
                vev = cVe.n
                cTe.n += 4
                ttv = cTe.n
                state["prev"] = ttv
                cS.n += 1
                sv = cS.n
                dview = vview_all[:, 4 * th:4 * th + 4, vo:vo + 128]
                srcv = u.rearrange("p (j c) -> p j c", c=128)
                isl = bc_inv[:, th * 512:(th + 1) * 512]

                def te_part(u2=u, v2=vt, tv2=tv, pv=prev, vev2=vev, ttv2=ttv,
                            sv2=sv, dv=dview, sv3=srcv, il=isl):
                    P.raw("vector", lambda eng: eng.wait_ge(cT.sem, tv2))
                    if pv is not None:
                        P.raw("vector", lambda eng: eng.wait_ge(cTe.sem, pv))
                    P.raw("vector", lambda eng: eng.tensor_mul(v2, u2, il).then_inc(cVe.sem, 1))
                    P.raw("tensor", lambda eng: eng.wait_ge(cVe.sem, vev2))
                    for jj in range(4):
                        P.raw("tensor", lambda eng, j2=jj: eng.matmul(
                            u2[:, j2 * 128:(j2 + 1) * 128],
                            v2[:, j2 * 128:(j2 + 1) * 128], c_id[:, :],
                            start=True, stop=True).then_inc(cTe.sem, 1))
                    P.raw("scalar", lambda eng: eng.wait_ge(cTe.sem, ttv2))
                    P.raw("scalar", lambda eng: eng.wait_ge(cS.sem, sv2 - 1))
                    P.raw("scalar", lambda eng: eng.copy(dv, sv3).then_inc(cS.sem, 1))

                defer(te_part)
                state["u"] += 1
                last_guard = (cS, sv)
            return last_guard

        return epi

    dag = Ctr(mksem("dag"), 1)
    ag_state = {}
    dst_h = lambda ct: buf_h[:, ct * 1024:(ct + 1) * 1024]
    src_own = lambda ct: h_xown[128 * ct:128 * (ct + 1), :]

    xsqAB = [xsqA, xsqB]
    te_sq = {}
    tvst = None
    for ct in range(NCT):
        xi, slot, xw = load_x(src_own(ct), 1024)
        P.wait("scalar", xw[0], xw[1])
        sv = P.on("scalar", lambda sc, sl=slot, c2=ct: sc.copy(
            buf_h[:, c2 * 1024:(c2 + 1) * 1024], xtw(sl, 1024)), inc=cS)
        x_done[xi] = (cS, sv)
        P.wait("vector", cS, sv)
        if ct >= 2:
            P.wait("vector", cT, te_sq[ct - 2])
        vv = P.on("vector", lambda v, c2=ct, xb=xsqAB[ct % 2]: v.tensor_mul(
            xb[:, :], buf_h[:, c2 * 1024:(c2 + 1) * 1024],
            buf_h[:, c2 * 1024:(c2 + 1) * 1024]), inc=cV)
        P.wait("tensor", cS, sv)
        for th in range(2):
            P.on("tensor", lambda t, c2=ct, t2=th: t.matmul(
                psd[2][0:1, t2 * 512:(t2 + 1) * 512], c_ocolb[:, :],
                buf_h[:, c2 * 1024 + t2 * 512: c2 * 1024 + (t2 + 1) * 512],
                start=(c2 == 0), stop=(c2 == 31)))
        P.wait("tensor", cV, vv)
        for th in range(2):
            tvst = P.on("tensor", lambda t, c2=ct, t2=th, xb=xsqAB[ct % 2]: t.matmul(
                psd[3][0:1, t2 * 512:(t2 + 1) * 512], c_ocolb[:, :],
                xb[:, t2 * 512:(t2 + 1) * 512],
                start=(c2 == 0), stop=(c2 == 31)), inc=cT)
        te_sq[ct] = tvst
    ring_gates[6] = x_done[28]
    ring_gates[7] = x_done[28]
    P.wait("scalar", dcst, CST)
    drain(ln_finalize(1024, psum_tv=tvst, mkstd=True))
    bc2a = holder_fin[0]
    P.wait("vector", dcst, CST)
    P.wait("vector", cS, bc2a)
    P.on("vector", lambda v: v.tensor_mul(c_cso[:, :], c_cso[:, :], bc_inv[:, :]))
    P.on("vector", lambda v: v.tensor_mul(c_sco[:, :], c_sco[:, :], bc_inv[:, :]))
    shift_vvA = []
    for ct in range(NCT):
        shift_vvA.append(P.on("vector", lambda v, c2=ct: v.tensor_add(
            buf_h[:, c2 * 1024:(c2 + 1) * 1024],
            buf_h[:, c2 * 1024:(c2 + 1) * 1024], bc_negm[:, :]), inc=cV))
    P.wait("tensor", dcst, CST)

    rhs = lambda ct, th: buf_h[:, ct * 1024 + th * 512: ct * 1024 + (th + 1) * 512]
    kq_epi = rope_factory(
        c_cso, c_sco,
        lambda og, th: th * 512,
        lambda og, ot, th: kstage[:, ((2 * og + ot) % 8) * 1024 + th * 512:
                                  ((2 * og + ot) % 8) * 1024 + (th + 1) * 512])
    v_epi = v_factory()
    kv_written = [0]

    def q_pre_evac(og):
        if og >= 12:
            if og < 16:
                return (dqf[og % 4], 16)
            return (qds[og % 4], 16 * ((og - 12) // 4))
        return (dscr, dscr.n)

    q_epi = rope_factory(
        c_cso, c_sco, lambda og, th: th * 512,
        lambda og, ot, th: kstage[:, ((2 * (og - 8) + ot) % 8) * 1024 + th * 512:
                                  ((2 * (og - 8) + ot) % 8) * 1024 + (th + 1) * 512],
        pre_evac_wait=q_pre_evac)

    def a_epilogue(og, units, tv, defer):
        if og < 4:
            g = kq_epi(og, units, tv, defer)
            ctr, val = g

            def kdma(og2=og, val2=val):
                P.raw("scalar", lambda eng: eng.wait_ge(cS.sem, val2))
                for ot in range(2):
                    hh = 2 * og2 + ot
                    P.raw("scalar", lambda eng, h2=hh: eng.dma_start(
                        s_kl[h2 * 128:(h2 + 1) * 128, :],
                        kstage[:, h2 * 1024:(h2 + 1) * 1024]).then_inc(dscr.sem, 16))
            dscr.n += 32
            kv_written[0] += 2
            defer(kdma)
            return g
        if og < 8:
            return v_epi(og, units, tv, defer)
        if og == 8:
            kd_total = dscr.n
            P.wait("scalar", cS, cS.n)
            for tt in range(8):
                P.on("scalar", lambda sc, t2=tt: sc.dma_start(
                    s_vl[t2 * 128:(t2 + 1) * 128, :],
                    vstage[:, t2 * 1024:(t2 + 1) * 1024]), inc=dvscr)
            ag_state["vd"] = dvscr.n
            P.wait("gpsimd", dscr, kd_total)
            ag_state["v1"] = P.on("gpsimd", lambda gg: gg.collective_compute(
                "AllGather", mybir.AluOpType.bypass,
                replica_groups=[[0, 1, 2, 3], [4, 5, 6, 7]],
                ins=[s_kl.opt()], outs=[s_ka.opt()]), inc=dag)
        if og == 12:
            P.wait("gpsimd", dvscr, ag_state["vd"])
            P.wait("gpsimd", dag, ag_state["v1"])
            P.on("gpsimd", lambda gg: gg.collective_compute(
                "AllGather", mybir.AluOpType.bypass,
                replica_groups=[[0, 1, 2, 3], [4, 5, 6, 7]],
                ins=[s_vl.opt()], outs=[s_va.opt()]), inc=dag)
        g = q_epi(og, units, tv, defer)
        ctr, val = g
        qsem = dqf[og - 8] if og < 12 else qds[og % 4]
        qsem.n += 16

        def qdma(og2=og, val2=val, qs=qsem):
            P.raw("scalar", lambda eng: eng.wait_ge(cS.sem, val2))
            h0 = 2 * (og2 - 8)
            src = kstage[:, (h0 % 8) * 1024:(h0 % 8) * 1024 + 2048].rearrange(
                "p (g2 t) -> p g2 t", t=1024)
            dst = s_q[h0 * 128:(h0 + 2) * 128, :].rearrange("(g2 p) t -> p g2 t", p=128)
            P.raw("scalar", lambda eng, d=dst, s2=src: eng.dma_start(d, s2).then_inc(qs.sem, 16))

        defer(qdma)
        return g

    proj_stream(h_wqkv, 24, rhs, (c_bqkv, lambda og, ot: og * 256 + ot * 128,
                                  lambda t2: c_std[0:1, t2 * 512:(t2 + 1) * 512]),
                a_epilogue, th_count=2, psd_mode="quad",
                init_guards={0: (cS, bc2a), 1: (cS, bc2a)},
                rhs_ready=[(cV, v) for v in shift_vvA])
    QSCR = dscr.n
    QD_TOTALS = [q.n for q in qds]
    AG_DONE = dag.n

    c_entry = (cT.n, cTe.n, cS.n, cV.n, cG.n,
               [d.n for d in dfsp], [d.n for d in dfbc])
    head_done = None
    branch_q = {pp: {e: [] for e in ENGS} for pp in range(4)}
    ends = []
    mark6 = {}
    for pp in range(4):
        cT.n, cTe.n, cS.n, cV.n, cG.n = c_entry[:5]
        for d, v in zip(dfsp, c_entry[5]):
            d.n = v
        for d, v in zip(dfbc, c_entry[6]):
            d.n = v
        P.cur = branch_q[pp]
        hd_local = []
        nlo, nhi = 4 * (pp + 1), 4 * (8 - pp)
        prev_exp = {}
        prev_pv = {}
        prev_attn = {}
        prev_denred = {}
        prev_sfree = [None]
        prev_mult_guard = {0: None, 1: None}
        fin_cnt = {0: 0, 1: 0, 2: 0, 3: 0}
        P.wait("tensor", cS, c_entry[2])
        P.wait("tensor", cV, c_entry[3])
        for h8 in range(8):
            slot = h8 % 2
            P.wait("tensor", dkvs[slot], 17 * 16 * (h8 // 2 + 1))
            for piece in range(2):
                qoff = piece * 512
                nk = nlo if piece == 0 else nhi
                pv_pending = None

                def emit_pv(ktp, dmap, nk2, slot2):
                    for g in range(2):
                        for q2 in range(2):
                            qh = 2 * g + q2
                            dctr, dval = dmap[qh]
                            P.wait("tensor", dctr, dval)
                            if ktp == 0 and prev_attn.get(qh) is not None:
                                P.wait("tensor", cV, prev_attn[qh])
                            pcol = (2 * g + ktp % 2) * 1024 + q2 * 512
                            tvp = P.on("tensor", lambda t, s2=slot2, k2=ktp, g3=g, q3=q2, pc=pcol,
                                       f=(ktp == 0), l=(ktp == nk2 - 1):
                                       t.matmul(psd[2 + g3][:, q3 * 512:(q3 + 1) * 512],
                                                vbuf[:, s2 * 4096 + k2 * 128: s2 * 4096 + (k2 + 1) * 128],
                                                pbuf[:, pc:pc + 512], start=f, stop=l), inc=cT)
                            prev_pv[(g, ktp % 2)] = tvp

                last_den = {}
                for kt in range(nk):
                    par = kt % 2
                    sv_tv = {}
                    for g in range(2):
                        pe = prev_exp.get(g)
                        if pe is not None:
                            P.wait("tensor", cS, pe)
                        if kt == 0 and prev_sfree[0] is not None:
                            P.wait("tensor", cS, prev_sfree[0][g])
                        for q2 in range(2):
                            qh = 2 * g + q2
                            qcol = qh * 1024 + qoff
                            sv_tv[g] = P.on(
                                "tensor", lambda t, s2=slot, k2=kt, g3=g, q3=q2, qc=qcol:
                                t.matmul(psd[g3][:, q3 * 512:(q3 + 1) * 512],
                                         kbuf[:, s2 * 4096 + k2 * 128: s2 * 4096 + (k2 + 1) * 128],
                                         qbuf[:, s2 * 4096 + qc: s2 * 4096 + qc + 512],
                                         start=True, stop=True), inc=cT)
                    if pv_pending is not None:
                        emit_pv(pv_pending[0], pv_pending[1], nk, slot)
                    for g in range(2):
                        P.wait("scalar", cT, sv_tv[g])
                        pk = prev_pv.get((g, par))
                        if pk is not None:
                            P.wait("scalar", cT, pk)
                        prev_exp[g] = P.on(
                            "scalar", lambda sc, g3=g, p2=par:
                            sc.activation(pbuf[:, (2 * g3 + p2) * 1024:(2 * g3 + p2 + 1) * 1024],
                                          psd[g3][:, :], AF.Exp, scale=RSCALE), inc=cS)
                    den_ms = {}
                    for g in range(2):
                        gcol = (2 * g + par) * 1024
                        if kt == nk - 1 and g == 1:
                            dlt = 128 * (kt - (nk - 4))
                            for q2 in range(2):
                                eng = "gpsimd" if q2 == 0 else "vector"
                                ctr = cG if q2 == 0 else cV
                                P.wait(eng, cS, prev_exp[g])
                                if q2 == 1 and last_den.get(1) is not None:
                                    P.wait(eng, last_den[1][0], last_den[1][1])
                                P.on(eng, lambda v, pc=gcol + q2 * 512, d2=dlt:
                                     v.tensor_mul(pbuf[:, pc:pc + 512], pbuf[:, pc:pc + 512],
                                                  c_mext[:, 384 - d2:896 - d2]))
                                v0 = P.on(eng, lambda v, pc=gcol + q2 * 512, g3=g, q3=q2:
                                          v.tensor_add(
                                              dacc[:, g3 * 1024 + q3 * 512:g3 * 1024 + (q3 + 1) * 512],
                                              dacc[:, g3 * 1024 + q3 * 512:g3 * 1024 + (q3 + 1) * 512],
                                              pbuf[:, pc:pc + 512]), inc=ctr)
                                den_ms[2 * g + q2] = (ctr, v0)
                            continue
                        eng = "gpsimd" if g == 1 else "vector"
                        ctr = cG if g == 1 else cV
                        P.wait(eng, cS, prev_exp[g])
                        if kt >= nk - 4:
                            dlt = 128 * (kt - (nk - 4))
                            for q2 in range(2):
                                P.on(eng, lambda v, pc=gcol + q2 * 512, d2=dlt:
                                     v.tensor_mul(pbuf[:, pc:pc + 512], pbuf[:, pc:pc + 512],
                                                  c_mext[:, 384 - d2:896 - d2]))
                        if kt == 0:
                            for q2 in range(2):
                                pd0 = prev_denred.get(2 * g + q2)
                                if pd0 is not None:
                                    P.wait(eng, pd0[0], pd0[1])
                            v0 = P.on(eng, lambda v, pc=gcol, g3=g:
                                      v.tensor_copy(dacc[:, g3 * 1024:(g3 + 1) * 1024],
                                                    pbuf[:, pc:pc + 1024]), inc=ctr)
                        else:
                            v0 = P.on(eng, lambda v, pc=gcol, g3=g:
                                      v.tensor_add(dacc[:, g3 * 1024:(g3 + 1) * 1024],
                                                   dacc[:, g3 * 1024:(g3 + 1) * 1024],
                                                   pbuf[:, pc:pc + 1024]), inc=ctr)
                        den_ms[2 * g] = (ctr, v0)
                        den_ms[2 * g + 1] = (ctr, v0)
                        last_den[g] = (ctr, v0)
                    pv_pending = (kt, den_ms)
                emit_pv(pv_pending[0], pv_pending[1], nk, slot)
                den_fin = pv_pending[1]
                pv_pending = None
                tv1 = {}
                for qh in range(4):
                    P.wait("tensor", den_fin[qh][0], den_fin[qh][1])
                    tv1[qh] = P.on("tensor", lambda t, q4=qh: t.matmul(
                        psd[q4 // 2][0:1, (q4 % 2) * 512:(q4 % 2 + 1) * 512], c_ocol[:, :],
                        dacc[:, q4 * 512:(q4 + 1) * 512], start=True, stop=True), inc=cT)
                P.wait("scalar", cT, tv1[1])
                s0 = P.on("scalar", lambda sc: sc.activation(
                    dacc[0:1, 0:1024], psd[0][0:1, 0:1024], AF.Ln), inc=cS)
                P.wait("scalar", cT, tv1[3])
                s1 = P.on("scalar", lambda sc: sc.activation(
                    dacc[0:1, 1024:2048], psd[1][0:1, 0:1024], AF.Ln), inc=cS)
                vexp = P.on("scalar", lambda sc: sc.activation(
                    dacc[0:1, 0:2048], dacc[0:1, 0:2048], AF.Exp, scale=-1.0), inc=cS)
                prev_sfree[0] = {0: s0, 1: s1}
                va_of = {}
                for qh in range(4):
                    P.wait("scalar", cS, vexp)
                    if fin_cnt[qh] > 0:
                        P.wait("scalar", dfbc[qh], 16 * fin_cnt[qh])
                    P.on("scalar", lambda sc, q4=qh: sc.dma_start(
                        s_fin[q4:q4 + 1, :], dacc[0:1, q4 * 512:(q4 + 1) * 512]), inc=dfsp[qh])
                    prev_denred[qh] = (dfsp[qh], dfsp[qh].n)
                    if qh >= 2:
                        P.wait("gpsimd", cV, va_of[qh - 2])
                    elif prev_mult_guard[qh % 2] is not None:
                        P.wait("gpsimd", cV, prev_mult_guard[qh % 2])
                    P.wait("gpsimd", dfsp[qh], dfsp[qh].n)
                    P.on("gpsimd", lambda gg, q4=qh: gg.dma_start(
                        invden[:, (q4 % 2) * 512:(q4 % 2 + 1) * 512],
                        bass.AP(s_fin.tensor, q4 * 512, [[0, 128], [1, 512]])), inc=dfbc[qh])
                    fin_cnt[qh] += 1
                    P.wait("vector", dfbc[qh], dfbc[qh].n)
                    hq_g = 4 * h8 + qh
                    va = P.on("vector", lambda v, q4=qh, hg=hq_g, qo=qoff:
                              v.tensor_mul(buf_at[:, hg * 1024 + qo: hg * 1024 + qo + 512],
                                           psd[2 + q4 // 2][:, (q4 % 2) * 512:(q4 % 2 + 1) * 512],
                                           invden[:, (q4 % 2) * 512:(q4 % 2 + 1) * 512]), inc=cV)
                    prev_attn[qh] = va
                    va_of[qh] = va
                prev_mult_guard[0] = va_of[2]
                prev_mult_guard[1] = va_of[3]
            hd_local.append(prev_attn[3])
            if h8 == 6:
                mark6[pp] = cT.n
        ends.append((cT.n, cTe.n, cS.n, cV.n, cG.n, [d.n for d in dfsp], [d.n for d in dfbc]))
        if pp == 0:
            head_done = hd_local
        else:
            assert hd_local == head_done, f"branch {pp} head milestones differ"
    assert all(e == ends[0] for e in ends), f"branch ends differ: {ends}"
    cT.n, cTe.n, cS.n, cV.n, cG.n = ends[0][:5]
    for d, v in zip(dfsp, ends[0][5]):
        d.n = v
    for d, v in zip(dfbc, ends[0][6]):
        d.n = v
    P.cur = P.q

    def piece_owner(j):
        return j if j < 4 else 7 - j

    def piece_loff(j):
        return 0 if j < 4 else 512

    def kload(h8, slot):
        for j in range(8):
            r, lo = piece_owner(j), piece_loff(j)
            P.on("sync", lambda s, h2=h8, s2=slot, j2=j, r2=r, l2=lo: s.dma_start(
                kbuf[:, s2 * 4096 + j2 * 512: s2 * 4096 + (j2 + 1) * 512],
                s_ka[r2 * 1024 + h2 * 128: r2 * 1024 + (h2 + 1) * 128, l2:l2 + 512]),
                inc=dkvs[slot])

    def vload(h8, slot):
        for j in range(8):
            r, lo = piece_owner(j), piece_loff(j)
            P.on("sync", lambda s, h2=h8, s2=slot, j2=j, r2=r, l2=lo: s.dma_start(
                vbuf[:, s2 * 4096 + j2 * 512: s2 * 4096 + (j2 + 1) * 512].rearrange(
                    "p (k d2) -> p k d2", d2=128),
                s_va[r2 * 1024 + l2: r2 * 1024 + l2 + 512,
                     h2 * 128:(h2 + 1) * 128].rearrange("(k p) d2 -> p k d2", p=128)),
                inc=dkvs[slot])

    def qload(h8, slot):
        P.on("sync", lambda s, h2=h8, s2=slot: s.dma_start(
            qbuf[:, s2 * 4096:(s2 + 1) * 4096].rearrange("p (g t) -> p g t", t=1024),
            s_q[h2 * 512:(h2 + 1) * 512, :].rearrange("(g p) t -> p g t", p=128)), inc=dkvs[slot])

    P.wait("sync", dag, AG_DONE)
    vload(0, 0)
    vload(1, 1)
    for i in range(4):
        P.wait("sync", dqf[i], 16)
    qload(0, 0)
    qload(1, 1)
    P.wait("sync", dscr, QSCR)
    for i, qn in enumerate(QD_TOTALS):
        P.wait("sync", qds[i], qn)
    kload(0, 0)
    kload(1, 1)
    for h8 in range(2, 8):
        slot = h8 % 2
        P.wait("sync", cV, head_done[h8 - 2])
        kload(h8, slot)
        vload(h8, slot)
        qload(h8, slot)

    def mk_branch(e):
        bodies = [branch_q[pp][e] for pp in range(4)]

        def fn(eng, bodies=bodies, nm=e):
            with eng.register(f"br_{nm}") as r, eng.register(f"brm_{nm}") as r2:
                eng.reg_mov(r, eng.partition_id())
                eng.reg_mod(r2, r, 4)
                with eng.If_eq(r2, 0):
                    for f in bodies[0]:
                        f(eng)
                with eng.Else():
                    with eng.If_eq(r2, 1):
                        for f in bodies[1]:
                            f(eng)
                    with eng.Else():
                        with eng.If_eq(r2, 2):
                            for f in bodies[2]:
                                f(eng)
                        with eng.Else():
                            for f in bodies[3]:
                                f(eng)
        P.q[e].append(fn)

    for e in ("tensor", "scalar", "vector", "gpsimd"):
        mk_branch(e)

    assert all(mark6[pp] == mark6[0] for pp in range(4)), mark6
    P.wait("gpsimd", cV, cV.n)
    P.wait("gpsimd", cS, cS.n)
    P.wait("gpsimd", cT, mark6[0])
    P.on("gpsimd", lambda g: g.dma_start(c_bo[:, :], h_bo[:, :]), inc=dcst)
    CSTD = dcst.n
    ring_gates[_ci[0] + 4] = (cT, cT.n)
    P.wait("tensor", dcst, CSTD)
    P.wait("tensor", cS, cS.n)
    P.wait("tensor", cV, cV.n)
    P.wait("vector", cS, cS.n)
    P.wait("sync", cV, cV.n)
    stats_first = [True]
    ln2_last = [None]
    sq_done = {}

    def wo_epilogue(og, units, tv, defer):
        guard = None
        for ot in range(2):
            otg = 2 * og + ot
            xi, slot, xw = load_x(h_xown[otg * 128:(otg + 1) * 128, :])
            xb = otg % 2
            pd = psd[2 * (og % 2) + ot]
            P.wait("vector", xw[0], xw[1])
            P.wait("vector", cT, tv)
            if otg >= 2:
                P.wait("vector", dx1s[xb], 16 * (otg // 2))
                P.wait("vector", cS, sq_done[otg - 2])
            vv = P.on("vector", lambda v, sl=slot, x2=xb, u=pd:
                      v.tensor_add(x1buf[:, x2 * 1024:(x2 + 1) * 1024], xt(sl), u[:, :]), inc=cV)
            x_done[xi] = (cV, vv)
            P.wait("scalar", cV, vv)
            P.on("scalar", lambda sc, x2=xb, o2=otg: sc.dma_start(
                s_x1[o2 * 128:(o2 + 1) * 128, :], x1buf[:, x2 * 1024:(x2 + 1) * 1024]), inc=dx1s[xb])
            P.on("scalar", lambda sc, x2=xb, o2=otg: sc.copy(
                buf_h[:, o2 * 1024:(o2 + 1) * 1024], x1buf[:, x2 * 1024:(x2 + 1) * 1024]))
            vs, ss = stat_into(x1buf[:, xb * 1024:(xb + 1) * 1024], stats_first[0])
            stats_first[0] = False
            sq_done[otg] = ss
            ln2_last[0] = vs
            guard = (cV, vs)
        return guard

    rhso = lambda ct, th: buf_at[:, ct * 1024 + th * 512: ct * 1024 + (th + 1) * 512]
    proj_stream(h_wo, 16, rhso, (c_bo, lambda og, ot: og * 256 + ot * 128), wo_epilogue)
    X1_TOTALS = [d.n for d in dx1s]

    P.wait("gpsimd", cT, cT.n)
    P.wait("gpsimd", cV, cV.n)
    for dst, src in [(c_lb, h_lb), (c_lga, h_lga)]:
        P.on("gpsimd", lambda g, d=dst, s=src: g.dma_start(d[:, :], s[:, :]), inc=dcst)
    CSTE = dcst.n
    P.wait("tensor", dcst, CSTE)
    P.wait("tensor", cV, ln2_last[0])
    drain(ln_finalize(1024))
    bc2 = holder_fin[0]
    P.wait("sync", dx1s[0], X1_TOTALS[0])
    P.wait("sync", dx1s[1], X1_TOTALS[1])
    P.wait("vector", cS, bc2)
    shift_vv = []
    for ct in range(NCT):
        shift_vv.append(P.on("vector", lambda v, c2=ct: v.tensor_add(
            buf_h[:, c2 * 1024:(c2 + 1) * 1024],
            buf_h[:, c2 * 1024:(c2 + 1) * 1024], bc_negm[:, :]), inc=cV))
    P.wait("tensor", cS, bc2)

    tv = None
    for ct in range(NCT):
        P.wait("tensor", cV, shift_vv[ct])
        for th in range(2):
            tv = P.on("tensor", lambda t, c2=ct, t2=th: t.matmul(
                psd[2][0:36, t2 * 512:(t2 + 1) * 512], c_lga[:, c2 * 36:(c2 + 1) * 36],
                buf_h[:, c2 * 1024 + t2 * 512: c2 * 1024 + (t2 + 1) * 512],
                start=(c2 == 0), stop=(c2 == 31)), inc=cT)
    P.wait("vector", cT, tv)
    vv = P.on("vector", lambda v: v.tensor_mul(
        gtmp[32:36, :], psd[2][32:36, :], bc_inv[32:36, :]), inc=cV)
    P.wait("scalar", cV, vv)
    sv = P.on("scalar", lambda sc: sc.activation(
        lgrow[32:36, :], gtmp[32:36, :], AF.Identity, bias=c_ebc[32:36, 0:1]), inc=cS)
    P.on("scalar", lambda sc: sc.dma_start(grows[0:1, 0:1024], lgrow[32:33, :]), inc=dgr)
    P.on("scalar", lambda sc: sc.dma_start(grows[0:1, 1024:2048], lgrow[33:34, :]), inc=dgr)
    P.wait("sync", cS, sv)
    P.on("sync", lambda s: s.dma_start(grows[0:1, 2048:3072], lgrow[34:35, :]), inc=dgr)
    P.on("sync", lambda s: s.dma_start(grows[0:1, 3072:4096], lgrow[35:36, :]), inc=dgr)
    P.wait("vector", dgr, dgr.n)
    P.on("vector", lambda v: v.tensor_max(grows[0:1, 4096:5120], grows[0:1, 0:1024], grows[0:1, 1024:2048]))
    P.on("vector", lambda v: v.tensor_max(grows[0:1, 5120:6144], grows[0:1, 2048:3072], grows[0:1, 3072:4096]))
    P.on("vector", lambda v: v.tensor_max(grows[0:1, 6144:7168], grows[0:1, 4096:5120], grows[0:1, 5120:6144]))
    vv_ise = None
    for e in range(4):
        vv_ise = P.on("vector", lambda v, e2=e: v.tensor_tensor(
            ohr[0:1, e2 * 1024:(e2 + 1) * 1024], grows[0:1, e2 * 1024:(e2 + 1) * 1024],
            grows[0:1, 6144:7168], ALU.is_equal), inc=cV)

    zvh = [None]
    ig = {0: None, 1: None}

    def gate_bg(vise):
        P.wait("tensor", cV, vise)
        tvh = None
        for e in range(4):
            for t2 in range(2):
                tvh = P.on("tensor", lambda t, e2=e, t3=t2: t.matmul(
                    psd[3][0:32, t3 * 512:(t3 + 1) * 512], c_eall[0:1, 32 * e2:32 * (e2 + 1)],
                    ohr[0:1, e2 * 1024 + t3 * 512: e2 * 1024 + (t3 + 1) * 512],
                    start=(e2 == 0), stop=(e2 == 3)), inc=cT)
        yield
        P.wait("scalar", cT, tvh)
        sv2 = P.on("scalar", lambda sc: sc.copy(m32[0:32, :], psd[3][0:32, :]), inc=cS)
        yield
        P.wait("vector", cS, sv2)
        zvh[0] = P.on("vector", lambda v: v.tensor_mul(
            zrow[0:32, :], psd[2][0:32, :], m32[0:32, :]), inc=cV)
        ig[1] = (cV, zvh[0])
        yield

    first_delta = [True]

    def delta_acc(og, ot, th, unit):
        otg = 2 * og + ot
        if first_delta[0]:
            first_delta[0] = False
            P.wait("tensor", cV, zvh[0])
        return P.on("tensor", lambda t, o2=otg, t2=th, u=unit: t.matmul(
            u, c_lb[0:32, o2 * 128:(o2 + 1) * 128],
            zrow[0:32, t2 * 512:(t2 + 1) * 512], start=False, stop=True), inc=cT)

    def up_epilogue(og, units, tv, defer):
        guard = None
        for ot in range(2):
            otg = 2 * og + ot
            xi, slot, xw = load_x(s_x1[otg * 128:(otg + 1) * 128, :])
            xb = otg % 2
            pd = psd[2 * (og % 2) + ot]
            P.wait("vector", cT, tv)
            vv0 = P.on("vector", lambda v, u=pd: v.tensor_mul(
                htmp[:, :], u[:, :], bc_inv[:, :]), inc=cV)
            P.wait("vector", xw[0], xw[1])
            if otg >= 2:
                P.wait("vector", douts[xb], 16 * (otg // 2))
            vv = P.on("vector", lambda v, sl=slot, x2=xb:
                      v.tensor_add(outb[:, x2 * 1024:(x2 + 1) * 1024], xt(sl), htmp[:, :]), inc=cV)
            x_done[xi] = (cV, vv)
            P.wait("sync", cV, vv)
            P.on("sync", lambda s, x2=xb, o2=otg: s.dma_start(
                h_out[o2 * 128:(o2 + 1) * 128, :], outb[:, x2 * 1024:(x2 + 1) * 1024]), inc=douts[xb])
            guard = (cV, vv0)
        return guard

    rhsu = lambda ct, th: buf_h[:, ct * 1024 + th * 512: ct * 1024 + (th + 1) * 512]
    proj_stream(h_wup, 16, rhsu, None, up_epilogue, extra_acc=delta_acc,
                bg=gate_bg(vv_ise), init_guards=ig)

    P.wait("sync", douts[0], douts[0].n)
    P.wait("sync", douts[1], douts[1].n)

    block = ctx.enter_context(nc.Block())
    for e in ENGS:
        def make(fn_list):
            def run(eng):
                for f in fn_list:
                    f(eng)
            return run
        getattr(block, e)(make(P.q[e]))
    ctx.close()
    return nc


def kernel(**inputs):
    per_core = _host_prep(inputs)
    nc = build_program()
    res = run_bass_kernel_spmd(nc, per_core, core_ids=list(range(8)))
    out = np.empty((NB, SEQ, D), dtype=np.float32)
    for c in range(8):
        b, p = c // 4, c % 4
        o = res.results[c]["outT"]
        out[b, 512 * p:512 * p + 512, :] = o[:, 0:512].T
        out[b, 512 * (7 - p):512 * (7 - p) + 512, :] = o[:, 512:1024].T
    return out
